# version 40
# baseline (speedup 1.0000x reference)
import numpy as np
import concourse.bass as bass
import concourse.mybir as mybir
from concourse.bass_utils import run_bass_kernel_spmd

AF = mybir.ActivationFunctionType
ALU = mybir.AluOpType
F32 = mybir.dt.float32
BF16 = mybir.dt.bfloat16
I32 = mybir.dt.int32
U8 = mybir.dt.uint8

S = 2048
E = 2048
NH = 16
DR = 64
DH = 128
HID = 5632
NHC = HID // 128
NEC = E // 128
TT = 512
NT = S // TT
QSCALE = float(1.0 / np.sqrt(DH))

ENGINES = ("t", "v", "s", "g", "q")


class Sched:

    def __init__(self):
        self.steps = {e: [] for e in ENGINES}
        self.counts = {}

    def st(self, eng, fn, deps=(), inc=None, amt=1):
        waits = []
        for d in deps:
            if d is None:
                continue
            if isinstance(d, list):
                waits += [x for x in d if x is not None]
            else:
                waits.append(d)
        tok = None
        if inc is not None:
            self.counts[inc] = self.counts.get(inc, 0) + amt
            tok = (inc, self.counts[inc])
        self.steps[eng].append((fn, waits, inc, amt))
        return tok

    def dma(self, eng, fn, deps=(), inc=None):
        return self.st(eng, fn, deps, inc, 16)

    def replay(self, eng_name, eng, sems):
        wm = {}
        for fn, waits, inc, amt in self.steps[eng_name]:
            for (sname, val) in waits:
                if wm.get(sname, 0) < val:
                    eng.wait_ge(sems[sname], val)
                    wm[sname] = val
            inst = fn(eng)
            if inc is not None:
                inst.then_inc(sems[inc], amt)


AR_N = 31744
AR_KH = (0, 2048)
AR_VT = (4096, 6144)
AR_ET = (8192, 9216)
AR_XDQ = 8192
AR_QA = 10240
AR_CAT = 18432
AR_SBI = 26624
AR_WO = (0, 2048)
AR_Y2 = 4096
AR_GS = (0, 2048)
AR_US = (4096, 6144)
AR_H = 8192
AR_DS = (0, 2816)
AR_GSIL = 30720


def build_program(nc):
    x_in = nc.dram_tensor("x", [E, S], F32, kind="ExternalInput")
    wdkv = nc.dram_tensor("wdkv", [128, NEC * 128], F32, kind="ExternalInput")
    wdq = nc.dram_tensor("wdq", [128, NEC * 128], F32, kind="ExternalInput")
    wkrp = nc.dram_tensor("wkrp", [128, NEC * 128], F32, kind="ExternalInput")
    wkrs = nc.dram_tensor("wkrs", [128, NEC * 128], F32, kind="ExternalInput")
    wqcr = nc.dram_tensor("wqcr", [128, NH * 128], F32, kind="ExternalInput")
    wqsw = nc.dram_tensor("wqsw", [128, NH * 128], F32, kind="ExternalInput")
    wuk = nc.dram_tensor("wuk", [128, NH * 64], F32, kind="ExternalInput")
    wuv = nc.dram_tensor("wuv", [128, NH * 128], F32, kind="ExternalInput")
    rtab = nc.dram_tensor("rtab", [128, 2 * S], F32, kind="ExternalInput")
    wo = nc.dram_tensor("wo", [NEC, 128, NEC * 128], F32, kind="ExternalInput")
    wg = nc.dram_tensor("wg", [NHC, 128, NEC * 128], U8, kind="ExternalInput")
    wu = nc.dram_tensor("wu", [NHC, 128, NEC * 128], U8, kind="ExternalInput")
    wd = nc.dram_tensor("wd", [NEC, 128, NHC * 128], U8, kind="ExternalInput")
    sg_in = nc.dram_tensor("sg", [128, NHC], F32, kind="ExternalInput")
    su_in = nc.dram_tensor("su", [128, NHC], F32, kind="ExternalInput")
    sd_in = nc.dram_tensor("sd", [128, NEC], F32, kind="ExternalInput")
    mks = nc.dram_tensor("mks", [128, 896], F32, kind="ExternalInput")
    idm = nc.dram_tensor("idm", [128, 128], F32, kind="ExternalInput")
    y_out = nc.dram_tensor("y", [E, S], F32, kind="ExternalOutput")

    wo_b = nc.dram_tensor("wo_b", [NEC, 128, NEC * 128], BF16, kind="Internal")

    sc = Sched()

    from contextlib import ExitStack
    with ExitStack() as ctx:
        def sb(name, shape, dt):
            return ctx.enter_context(nc.sbuf_tensor(name, shape, dt))

        def ps(name):
            return ctx.enter_context(nc.psum_tensor(name, [128, 512], F32))

        A = sb("A", [128, 8192], F32)
        B = sb("B", [128, 8192], F32)
        C = sb("C", [128, 8192], BF16)
        AR = sb("AR", [128, AR_N], BF16)
        ckv = sb("ckv", [128, S], BF16)
        krc = sb("krc", [128, S], BF16)
        rtabb = sb("rtabb", [128, 2 * S], BF16)
        mkb = sb("mkb", [128, 896], BF16)
        idb = sb("idb", [128, 128], BF16)
        w_dkv = sb("w_dkv", [128, NEC * 128], BF16)
        w_dq = sb("w_dq", [128, NEC * 128], BF16)
        w_krp = sb("w_krp", [128, NEC * 128], BF16)
        w_krs = sb("w_krs", [128, NEC * 128], BF16)
        w_qcr = sb("w_qcr", [128, NH * 128], BF16)
        w_qsw = sb("w_qsw", [128, NH * 128], BF16)
        w_uk = sb("w_uk", [128, NH * 64], BF16)
        w_uv = sb("w_uv", [128, NH * 128], BF16)
        sgT = sb("sgT", [128, NHC], F32)
        suT = sb("suT", [128, NHC], F32)
        sdT = sb("sdT", [128, NEC], F32)
        rows = sb("rows", [1, 1536], F32)
        onesb = sb("onesb", [128, 1], BF16)
        onesf = sb("onesf", [1, 128], F32)
        onesf2 = sb("onesf2", [128, 1], F32)
        m128b = sb("m128b", [1, 128], BF16)
        rowb = sb("rowb", [1, 512], BF16)
        SA = ctx.enter_context(nc.psum_tensor("SA", [128, 1024], F32))
        SB = ctx.enter_context(nc.psum_tensor("SB", [128, 1024], F32))
        O0, O1, SM, KV = ps("O0"), ps("O1"), ps("SM"), ps("KV")
        sem_names = ["T", "V", "S", "G", "Dx0", "Dx1", "Dx2", "Dx3", "Dsw0", "Dsw1", "Dsc",
                     "Dst0", "Dst1",
                     "Dwo0", "Dwo1", "Dg0", "Dg1", "Du0", "Du1",
                     "Dd0", "Dd1", "Dd2", "Dd3", "Dy"]
        sems = {n: ctx.enter_context(nc.semaphore("sem" + n))
                for n in sem_names}
        SA0, SA1 = SA[:, 0:512], SA[:, 512:1024]
        SB0, SB1 = SB[:, 0:512], SB[:, 512:1024]
        sbinv = AR[:, AR_SBI:AR_SBI + 1024].bitcast(F32)

        def MM(psum_ap, lhsT_ap, rhs_ap, start, stop, deps, inc=None):
            return sc.st(
                "t",
                lambda t, o=psum_ap, l=lhsT_ap, r=rhs_ap, a=start, b=stop:
                    t.matmul(o, l, r, start=a, stop=b),
                deps=deps, inc=inc)

        t_ones_b = sc.st("v", lambda v: v.memset(onesb[:], 1.0), inc="V")
        t_ones_f = sc.st("v", lambda v: v.memset(onesf[:], 1.0), inc="V")
        t_ones_f2 = sc.st("v", lambda v: v.memset(onesf2[:], 1.0), inc="V")
        t_m128 = sc.st("v", lambda v: v.memset(m128b[:], -128.0), inc="V")
        sc.dma("q", lambda q: q.dma_start(sgT[:], sg_in.ap()), inc="Dsc")
        sc.dma("q", lambda q: q.dma_start(suT[:], su_in.ap()), inc="Dsc")
        t_sd = ("Dsc", 48)
        sc.dma("q", lambda q: q.dma_start(sdT[:], sd_in.ap()), inc="Dsc")

        small_w = [
            (rtab, rtabb[:, :], 4096), (mks, mkb[:, :], 896),
            (idm, idb[:, :], 128),
            (wdkv, w_dkv[:, :], 2048), (wdq, w_dq[:, :], 2048),
            (wkrp, w_krp[:, :], 2048), (wkrs, w_krs[:, :], 2048),
            (wqcr, w_qcr[:, :], 2048), (wqsw, w_qsw[:, :], 2048),
            (wuk, w_uk[:, :], 1024), (wuv, w_uv[:, :], 2048),
        ]
        half_free = [None, None]
        for i, (src, dst, n) in enumerate(small_w):
            hb = i % 2
            off = hb * 4096
            t_ld = sc.dma(
                "q",
                lambda q, src=src, off=off, n=n: q.dma_start(
                    B[:, off:off + n], src.ap()),
                deps=[half_free[hb]], inc="Dsw%d" % hb)
            t_cv = sc.st(
                "v",
                lambda v, dst=dst, off=off, n=n: v.tensor_copy(
                    dst, B[:, off:off + n]),
                deps=[t_ld], inc="V")
            half_free[hb] = t_cv

        store_tok = {}
        for ec in range(NEC):
            hb = ec % 2
            off = hb * 4096
            t_ld = sc.dma(
                "q",
                lambda q, ec=ec, off=off: q.dma_start(
                    B[:, off:off + 2048], wo[ec, :, :]),
                deps=[half_free[hb]], inc="Dsw%d" % hb)
            cto = AR_CAT + hb * 2048
            t_cv = sc.st(
                "s",
                lambda s, off=off, cto=cto: s.activation(
                    AR[:, cto:cto + 2048], B[:, off:off + 2048],
                    AF.Copy),
                deps=[t_ld], inc="S")
            t_stb = sc.dma(
                "s",
                lambda s, ec=ec, cto=cto: s.dma_start(
                    wo_b[ec, :, :], AR[:, cto:cto + 2048]),
                deps=[t_cv], inc="Dst%d" % hb)
            half_free[hb] = t_stb
            store_tok.setdefault(("wo_b", ec), []).append(t_stb)


        prep_done = [t for t in half_free if t is not None] + [
            t_sd, t_ones_b, t_ones_f, t_ones_f2, t_m128,
            ("Dst0", sc.counts.get("Dst0", 0)),
            ("Dst1", sc.counts.get("Dst1", 0))]

        tok_A_free = list(prep_done)
        tok_B_free = list(prep_done)
        tok_C_free = list(prep_done)
        tok_pb = {k: None for k in
                  ("SA0", "SA1", "SB0", "SB1", "O0", "O1", "SM", "KV")}
        tok_cat_free = [[("Dst0", sc.counts.get("Dst0", 0)),
                         ("Dst1", sc.counts.get("Dst1", 0))]] * NH
        tok_kh_free = [None, None]
        tok_vtm_free = [None, None]
        tok_qall_free = None
        tok_rows_free = []
        tok_sbinv_free = None
        tok_sc_rd = [None, None]
        tok_expt_rd = [None, None]
        tok_rowb_free = None
        t_hc_last = None

        kv_tok = [[], []]
        carry = {}
        scrA = AR[64:128, 0:512]
        scrB = AR[64:128, 512:1024]
        kh_waits_init = [None]

        def make_kv_items(hh, ctx2, cache_w, q_ev):
            par = hh % 2
            khs = AR_KH[par]
            vts = AR_VT[par]
            nc52 = ctx2 // 512
            items = []

            def k_piece(c5, hh=hh, par=par, khs=khs):
                t_kn = MM(KV[0:64, :], w_uk[:, hh * 64:(hh + 1) * 64],
                          ckv[:, c5 * 512:(c5 + 1) * 512], True, True,
                          [tok_pb["KV"]] + (cache_w + q_ev
                                            if hh <= 1 and c5 == 0 else []),
                          inc="T")
                t_kn_e = sc.st(
                    "s",
                    lambda s, khs=khs, c5=c5: s.copy(
                        AR[0:64, khs + c5 * 512:khs + (c5 + 1) * 512],
                        KV[0:64, :]),
                    deps=[t_kn, tok_kh_free[par]], inc="S")
                tok_pb["KV"] = t_kn_e
                kv_tok[par].append(t_kn_e)

            def kr_piece(hh=hh, par=par, khs=khs, ctx2=ctx2):
                t_krcp = sc.st(
                    "v",
                    lambda v, khs=khs, ctx2=ctx2: v.tensor_copy(
                        AR[64:128, khs:khs + ctx2], krc[64:128, 0:ctx2]),
                    deps=[tok_kh_free[par]] + cache_w
                         + ([q_ev[-1]] if hh <= 1 else []), inc="V")
                kv_tok[par].append(t_krcp)

            def v_piece(tg, hh=hh, par=par, vts=vts):
                t_v = None
                for i in range(4):
                    tc = tg * 4 + i
                    t_v = MM(KV[:, i * 128:(i + 1) * 128],
                             ckv[:, tc * 128:(tc + 1) * 128],
                             w_uv[:, hh * 128:(hh + 1) * 128],
                             True, True,
                             [tok_pb["KV"]] if i == 0 else [],
                             inc="T" if i == 3 else None)
                t_v_e = sc.st(
                    "v",
                    lambda v, vts=vts, tg=tg: v.tensor_copy(
                        AR[:, vts + tg * 512:vts + (tg + 1) * 512],
                        KV[:, :]),
                    deps=[t_v, tok_vtm_free[par]], inc="V")
                tok_pb["KV"] = t_v_e
                kv_tok[par].append(t_v_e)

            for c5 in range(nc52):
                items.append(lambda c5=c5: k_piece(c5))
            items.append(kr_piece)
            for tg in range(nc52):
                items.append(lambda tg=tg: v_piece(tg))
            return items

        def emit_xloads(ti2, base_deps):
            q02 = ti2 * TT
            toks = []
            for k in range(4):
                toks.append(sc.dma(
                    "g",
                    lambda g, q02=q02, k=k: g.dma_start(
                        C[:, k * 2048:(k + 1) * 2048],
                        bass.AP(x_in, q02 + 4 * k * 128 * 2048,
                                [[2048, 128], [128 * 2048, 4], [1, 512]])),
                    deps=base_deps + ([toks[k - 2]] if k >= 2 else []),
                    inc="Dx%d" % (k % 2)))
            return toks

        def emit_astart(ti2, xw, dep_hz, dep_scr):
            q02 = ti2 * TT
            for kc in range(NEC):
                d = [tok_pb["SA0"]] if kc == 0 else []
                d += [xw[kc // 4]] if kc % 4 == 0 else []
                t_ckv = MM(SA0, w_dkv[:, kc * 128:(kc + 1) * 128],
                           C[:, kc * 512:(kc + 1) * 512],
                           kc == 0, kc == NEC - 1, d,
                           inc="T" if kc == NEC - 1 else None)
            for kc in range(NEC):
                d = [tok_pb["SA1"]] if kc == 0 else []
                t_xdq = MM(SA1, w_dq[:, kc * 128:(kc + 1) * 128],
                           C[:, kc * 512:(kc + 1) * 512],
                           kc == 0, kc == NEC - 1, d,
                           inc="T" if kc == NEC - 1 else None)
            for kc in range(NEC):
                d = [tok_pb["SB0"]] if kc == 0 else []
                t_krp = MM(SB0, w_krp[:, kc * 128:(kc + 1) * 128],
                           C[:, kc * 512:(kc + 1) * 512],
                           kc == 0, kc == NEC - 1, d,
                           inc="T" if kc == NEC - 1 else None)
            for kc in range(NEC):
                d = [tok_pb["SB1"]] if kc == 0 else []
                t_krs = MM(SB1, w_krs[:, kc * 128:(kc + 1) * 128],
                           C[:, kc * 512:(kc + 1) * 512],
                           kc == 0, kc == NEC - 1, d,
                           inc="T" if kc == NEC - 1 else None)
            c_last = t_krs

            t_ckv_e = sc.st(
                "s",
                lambda s, q02=q02: s.copy(ckv[:, q02:q02 + 512], SA0),
                deps=[t_ckv], inc="S")
            tok_pb["SA0"] = t_ckv_e
            t_xdq_e = sc.st(
                "s",
                lambda s: s.copy(AR[:, AR_XDQ:AR_XDQ + 512], SA1),
                deps=[t_xdq, tok_qall_free, tok_sc_rd[0]] + dep_hz, inc="S")
            tok_pb["SA1"] = t_xdq_e

            ccl = rtabb[64:128, q02:q02 + 512]
            ssl = rtabb[64:128, S + q02:S + q02 + 512]
            t_k1 = sc.st(
                "v",
                lambda v, ccl=ccl: v.tensor_tensor(
                    scrA, SB[64:128, 0:512], ccl, ALU.mult),
                deps=[t_krp] + dep_scr, inc="V")
            t_k2 = sc.st(
                "v",
                lambda v, ssl=ssl: v.tensor_tensor(
                    scrB, SB[64:128, 512:1024], ssl, ALU.mult),
                deps=[t_krs], inc="V")
            t_k3 = sc.st(
                "v",
                lambda v, q02=q02: v.tensor_tensor(
                    krc[64:128, q02:q02 + 512], scrA, scrB, ALU.add),
                deps=[t_k1, t_k2], inc="V")
            tok_pb["SB0"] = t_k1
            tok_pb["SB1"] = t_k2
            cache_w = [t_ckv_e, t_k3]

            t_q_ev = []
            pb_rd = {"O0": [tok_pb["O0"]], "O1": [tok_pb["O1"]],
                     "SM": [tok_pb["SM"]], "KV": [tok_pb["KV"]]}
            prev_add = None
            for h in range(NH):
                na = "O0" if h % 2 == 0 else "O1"
                nb = "SM" if h % 2 == 0 else "KV"
                Pa = O0 if h % 2 == 0 else O1
                Pb = SM if h % 2 == 0 else KV
                t_q1 = MM(Pa[:, :], w_qcr[:, h * 128:(h + 1) * 128],
                          AR[:, AR_XDQ:AR_XDQ + 512], True, True,
                          [t_xdq_e] + pb_rd[na], inc="T")
                t_q2 = MM(Pb[:, :], w_qsw[:, h * 128:(h + 1) * 128],
                          AR[:, AR_XDQ:AR_XDQ + 512], True, True,
                          pb_rd[nb], inc="T")
                t_qn_e = sc.st(
                    "s",
                    lambda s, h=h, b=Pa: s.copy(
                        AR[0:64, AR_QA + h * 512:AR_QA + h * 512 + 512],
                        b[0:64, :]),
                    deps=[t_q1, tok_qall_free], inc="S")
                t_m1 = sc.st(
                    "v",
                    lambda v, ccl=ccl, b=Pa: v.tensor_tensor(
                        scrA, b[64:128, :], ccl, ALU.mult),
                    deps=[t_q1, prev_add, t_k3 if h == 0 else None],
                    inc="V")
                t_m2 = sc.st(
                    "v",
                    lambda v, ssl=ssl, b=Pb: v.tensor_tensor(
                        scrB, b[64:128, :], ssl, ALU.mult),
                    deps=[t_q2], inc="V")
                t_add = sc.st(
                    "v",
                    lambda v, h=h: v.tensor_tensor(
                        AR[64:128, AR_QA + h * 512:AR_QA + h * 512 + 512],
                        scrA, scrB, ALU.add),
                    deps=[t_m1, t_m2, tok_qall_free], inc="V")
                prev_add = t_add
                pb_rd[na] = [t_qn_e, t_m1]
                pb_rd[nb] = [t_m2]
                t_q_ev += [t_qn_e, t_add]
            for k2 in ("O0", "O1", "SM", "KV"):
                tok_pb[k2] = pb_rd[k2]

            ctx2 = (ti2 + 1) * TT
            for hh in (0, 1):
                for it in make_kv_items(hh, ctx2, cache_w, t_q_ev):
                    it()
            carry["ast"] = {"q_ev": t_q_ev, "cache_w": cache_w,
                            "ccl": ccl, "ssl": ssl, "c_last": c_last}

        for ti in range(NT):
            q0 = ti * TT
            ctx = (ti + 1) * TT
            nc5 = ctx // 512
            ntc = ctx // 128
            npair = ntc // 2

            if ti == 0:
                xw0 = emit_xloads(0, list(prep_done))
                emit_astart(0, xw0, [tok_qall_free], kh_waits_init)
            ast = carry.pop("ast")
            t_q_ev = ast["q_ev"]
            tok_cache_w = ast["cache_w"]
            ccl = ast["ccl"]
            ssl = ast["ssl"]
            tok_C_free = [ast["c_last"]]

            t_xr = []
            for j in range(2):
                t_xr.append(sc.dma(
                    "q",
                    lambda q, q0=q0, j=j: q.dma_start(
                        A[:, j * 4096:(j + 1) * 4096],
                        bass.AP(x_in, q0 + 8 * j * 128 * 2048,
                                [[2048, 128], [128 * 2048, 8], [1, 512]])),
                    deps=tok_A_free, inc="Dx%d" % (2 + j)))

            pending_tail = None
            pending_tail_b = None
            st8 = {"rows": tok_rows_free, "sbinv": tok_sbinv_free,
                   "si": None}
            for h in range(NH):
                kb = h % 2
                oname = "O0" if kb == 0 else "O1"
                O = O0 if kb == 0 else O1
                khs = AR_KH[kb]
                vts = AR_VT[kb]
                pending = (make_kv_items(h + 1, ctx, tok_cache_w, t_q_ev)
                           if 1 <= h < NH - 1 else [])
                first_deps = kv_tok[kb]
                kv_tok[kb] = []

                t_av = None
                t_sum = None
                t_exp = [None, None]
                for tp in range(npair + 1):
                    if tp < npair:
                        sb = tp % 2
                        SC = SA if sb == 0 else SB
                        etb = AR_ET[sb]
                        for i in range(2):
                            tc = 2 * tp + i
                            d = [tok_sc_rd[sb]] + first_deps
                            first_deps = []
                            diag = tc >= 4 * ti
                            t_scm = MM(SC[:, i * 512:(i + 1) * 512],
                                       AR[:, khs + tc * 128:khs + (tc + 1) * 128],
                                       AR[:, AR_QA + h * 512:AR_QA + (h + 1) * 512],
                                       True, not diag, d,
                                       inc=None if diag else "T")
                            if diag:
                                kk = 384 - 128 * (tc - 4 * ti)
                                t_scm = MM(SC[:, i * 512:(i + 1) * 512],
                                           idb[:, :],
                                           mkb[:, kk:kk + 512],
                                           False, True, [], inc="T")
                        t_exp[sb] = sc.st(
                            "s",
                            lambda s, SC=SC, etb=etb: s.activation(
                                AR[:, etb:etb + 1024], SC[:, 0:1024], AF.Exp,
                                scale=np.float32(QSCALE)),
                            deps=[t_scm, tok_expt_rd[sb]], inc="S")
                        tok_sc_rd[sb] = t_exp[sb]
                        if tp == 0 and pending_tail is not None:
                            pending_tail()
                            pending_tail = None
                    pa = tp - 1
                    if pa >= 0:
                        sbp = pa % 2
                        etbp = AR_ET[sbp]
                        for i in range(2):
                            tc = 2 * pa + i
                            d_av = [t_exp[sbp]]
                            if tc == 0:
                                d_av += [tok_pb[oname]]
                            t_av = MM(O[:, :],
                                      AR[:, vts + tc * 128:vts + (tc + 1) * 128],
                                      AR[:, etbp + i * 512:etbp + (i + 1) * 512],
                                      tc == 0, tc == ntc - 1, d_av, inc="T")
                        if pa == 0 and pending_tail_b is not None:
                            pending_tail_b()
                            pending_tail_b = None
                        for i in range(2):
                            tc = 2 * pa + i
                            d_sm = [tok_pb["SM"]] if tc == 0 else []
                            t_sum = MM(SM[0:1, :], onesb[:, :],
                                       AR[:, etbp + i * 512:etbp + (i + 1) * 512],
                                       tc == 0, tc == ntc - 1, d_sm, inc="T")
                        tok_expt_rd[sbp] = t_sum
                    if pending:
                        nleft = npair + 1 - tp
                        npop = (len(pending) + nleft - 1) // nleft
                        for _ in range(npop):
                            if pending:
                                pending.pop(0)()
                while pending:
                    pending.pop(0)()
                tok_kh_free[kb] = t_scm
                tok_vtm_free[kb] = t_av

                def mk_tail_a(t_sum=t_sum):
                    def tail_a():
                        t_ln = sc.st(
                            "s",
                            lambda s: s.activation(rows[0:1, 512:1024],
                                                   SM[0:1, :], AF.Ln),
                            deps=[t_sum] + st8["rows"], inc="S")
                        t_si = sc.st(
                            "s",
                            lambda s: s.activation(rows[0:1, 1024:1536],
                                                   rows[0:1, 512:1024],
                                                   AF.Exp,
                                                   scale=np.float32(-1.0)),
                            deps=[t_ln], inc="S")
                        st8["si"] = t_si
                    return tail_a

                def mk_tail_b(h=h, O=O, oname=oname, t_av=t_av):
                    def tail_b():
                        t_bc = MM(SM[:, :], onesf[0:1, :],
                                  rows[0:1, 1024:1536],
                                  True, True, [st8["si"]], inc="T")
                        st8["rows"] = [t_bc]
                        t_sbe = sc.st(
                            "v", lambda v: v.tensor_copy(sbinv, SM[:, :]),
                            deps=[t_bc, st8["sbinv"]], inc="V")
                        tok_pb["SM"] = t_sbe
                        t_mu = sc.st(
                            "v",
                            lambda v: v.tensor_tensor(
                                AR[:, AR_CAT + h * 512:
                                   AR_CAT + (h + 1) * 512],
                                O[:, :], sbinv, ALU.mult),
                            deps=[t_av, t_sbe, tok_cat_free[h]], inc="V")
                        tok_pb[oname] = t_mu
                        st8["sbinv"] = t_mu
                        tok_cat_free[h] = t_mu
                    return tail_b

                pending_tail = mk_tail_a()
                pending_tail_b = mk_tail_b()

            pending_tail()
            pending_tail_b()
            pending_tail = None
            pending_tail_b = None
            tok_rows_free = st8["rows"]
            tok_sbinv_free = st8["sbinv"]
            tok_qall_free = t_scm

            t_wo_free = [None, None]
            t_w2 = []
            for ec in range(NEC):
                wb = ec % 2
                dd = [t_wo_free[wb], tok_kh_free[wb]]
                if ti == 0:
                    dd += store_tok[("wo_b", ec)]
                t_ld = sc.dma(
                    "q",
                    lambda q, ec=ec, wb=wb: q.dma_start(
                        AR[:, AR_WO[wb]:AR_WO[wb] + 2048], wo_b[ec, :, :]),
                    deps=dd, inc="Dwo%d" % wb)
                wn = "SA0" if ec % 2 == 0 else "SB0"
                pbw = SA0 if ec % 2 == 0 else SB0
                for k in range(NEC):
                    d = [t_ld, tok_cat_free[k]]
                    if k == 0:
                        d += [tok_pb[wn], tok_sc_rd[ec % 2]]
                    t_mm = MM(pbw,
                              AR[:, AR_WO[wb] + k * 128:AR_WO[wb] + (k + 1) * 128],
                              AR[:, AR_CAT + k * 512:AR_CAT + (k + 1) * 512],
                              k == 0, k == NEC - 1, d,
                              inc="T" if k == NEC - 1 else None)
                t_wo_free[wb] = t_mm
                t_add = sc.st(
                    "v",
                    lambda v, ec=ec, pbw=pbw: v.tensor_tensor(
                        B[:, ec * 512:(ec + 1) * 512], pbw,
                        A[:, ec * 512:(ec + 1) * 512], ALU.add),
                    deps=[t_mm, t_xr[ec // 8]]
                         + (tok_B_free if ec == 0 else []), inc="V")
                tok_pb[wn] = t_add
                t_w2.append(t_add)

            def layernorm(src_f32, dst_f32, t_src_ready, y2_off, y2_free,
                          cast_out, sa_free, sb_free,
                          out_in_src=False, f32_stats=False, wide=True):
                t_cast = []
                t_sq = []
                for j in range(4):
                    dj = (t_src_ready[4 * j:4 * j + 4]
                          if len(t_src_ready) == 16 else t_src_ready)
                    if not f32_stats:
                        t_cast.append(sc.st(
                            "s",
                            lambda s, j=j: s.activation(
                                C[:, j * 2048:(j + 1) * 2048],
                                src_f32[:, j * 2048:(j + 1) * 2048], AF.Copy),
                            deps=dj + (tok_C_free if j == 0 else []),
                            inc="S"))
                    t_sq.append(sc.st(
                        "s",
                        lambda s, j=j: s.activation(
                            AR[:, y2_off + j * 2048:y2_off + (j + 1) * 2048],
                            src_f32[:, j * 2048:(j + 1) * 2048], AF.Square),
                        deps=dj + (y2_free if j == 0 else []), inc="S"))
                for kc in range(NEC):
                    if f32_stats:
                        d1 = ([t_src_ready[kc]] if len(t_src_ready) == 16
                              else list(t_src_ready) if kc == 0 else [])
                        if kc == 0:
                            d1 += [tok_pb["SM"]]
                        t_st1 = MM(SM[0:1, :], onesf2[:, :],
                                   src_f32[:, kc * 512:(kc + 1) * 512],
                                   kc == 0, kc == NEC - 1, d1,
                                   inc="T" if kc == NEC - 1 else None)
                        continue
                    d1 = [t_cast[kc // 4]] if kc % 4 == 0 else []
                    if kc == 0:
                        d1 += [tok_pb["SM"]]
                    t_st1 = MM(SM[0:1, :], onesb[:, :],
                               C[:, kc * 512:(kc + 1) * 512],
                               kc == 0, kc == NEC - 1, d1,
                               inc="T" if kc == NEC - 1 else None)
                for kc in range(NEC):
                    d2 = [t_sq[kc // 4]] if kc % 4 == 0 else []
                    if kc == 0:
                        d2 += [tok_pb["KV"]]
                    t_st2 = MM(KV[0:1, :], onesb[:, :],
                               AR[:, y2_off + kc * 512:y2_off + (kc + 1) * 512],
                               kc == 0, kc == NEC - 1, d2,
                               inc="T" if kc == NEC - 1 else None)
                t_mean = sc.st(
                    "v",
                    lambda v: v.tensor_scalar(rows[0:1, 0:512], SM[0:1, :],
                                              1.0 / E, None, ALU.mult),
                    deps=[t_st1] + tok_rows_free, inc="V")
                tok_pb["SM"] = t_mean
                t_ey2 = sc.st(
                    "v",
                    lambda v: v.tensor_scalar(rows[0:1, 512:1024],
                                              KV[0:1, :],
                                              1.0 / E, None, ALU.mult),
                    deps=[t_st2], inc="V")
                tok_pb["KV"] = t_ey2
                if wide:
                    t_bcm = [MM(SA0, onesf[0:1, :], rows[0:1, 0:512],
                                True, True, [t_mean] + sa_free, inc="T"),
                             MM(SA1, onesf[0:1, :], rows[0:1, 0:512],
                                True, True, [], inc="T")]
                else:
                    t_bcm = [MM(O0[:, :], onesf[0:1, :], rows[0:1, 0:512],
                                True, True, [t_mean] + sa_free, inc="T")]
                t_m2 = sc.st(
                    "s",
                    lambda s: s.activation(rows[0:1, 1024:1536],
                                           rows[0:1, 0:512], AF.Square),
                    deps=[t_mean], inc="S")
                t_var = sc.st(
                    "v",
                    lambda v: v.tensor_tensor(rows[0:1, 0:512],
                                              rows[0:1, 512:1024],
                                              rows[0:1, 1024:1536],
                                              ALU.subtract),
                    deps=[t_ey2, t_m2] + t_bcm, inc="V")
                t_lnv = sc.st(
                    "s",
                    lambda s: s.activation(rows[0:1, 512:1024],
                                           rows[0:1, 0:512], AF.Ln,
                                           bias=np.float32(1e-5)),
                    deps=[t_var], inc="S")
                t_rstd = sc.st(
                    "s",
                    lambda s: s.activation(rows[0:1, 1024:1536],
                                           rows[0:1, 512:1024], AF.Exp,
                                           scale=np.float32(-0.5)),
                    deps=[t_lnv], inc="S")
                if wide:
                    t_bcr = [MM(SB0, onesf[0:1, :], rows[0:1, 1024:1536],
                                True, True, [t_rstd] + sb_free, inc="T"),
                             MM(SB1, onesf[0:1, :], rows[0:1, 1024:1536],
                                True, True, [], inc="T")]
                else:
                    t_bcr = [MM(O1[:, :], onesf[0:1, :], rows[0:1, 1024:1536],
                                True, True, [t_rstd] + sb_free, inc="T")]
                t_norm = []
                t_s1 = None
                out_t = src_f32 if out_in_src else dst_f32
                for ec2 in range(NEC):
                    if wide:
                        sab = SA0 if ec2 % 2 == 0 else SA1
                        sbb = SB0 if ec2 % 2 == 0 else SB1
                    else:
                        sab, sbb = O0[:, :], O1[:, :]
                    d = t_bcm + t_bcr if ec2 == 0 else []
                    t_s1 = sc.st(
                        "v",
                        lambda v, ec2=ec2, sab=sab: v.tensor_tensor(
                            dst_f32[:, ec2 * 512:(ec2 + 1) * 512],
                            src_f32[:, ec2 * 512:(ec2 + 1) * 512],
                            sab, ALU.subtract),
                        deps=d, inc="V")
                    t_s2 = sc.st(
                        "v",
                        lambda v, ec2=ec2, sbb=sbb, out_t=out_t: v.tensor_tensor(
                            out_t[:, ec2 * 512:(ec2 + 1) * 512],
                            dst_f32[:, ec2 * 512:(ec2 + 1) * 512],
                            sbb, ALU.mult),
                        deps=[t_s1], inc="V")
                    t_norm.append(t_s2)
                t_co = None
                if cast_out:
                    t_co = []
                    for j in range(4):
                        t_co.append(sc.st(
                            "s",
                            lambda s, j=j: s.activation(
                                C[:, j * 2048:(j + 1) * 2048],
                                dst_f32[:, j * 2048:(j + 1) * 2048],
                                AF.Copy),
                            deps=t_norm[4 * j:4 * j + 4], inc="S"))
                return t_norm, t_co, t_sq[-1], t_bcm + t_bcr, t_st2, t_s1

            t_norm1, t_y16s, t_sq1, rows_rd1, _, t_sub1 = layernorm(
                B, A, t_w2, AR_H, [], cast_out=True,
                sa_free=[tok_pb["SA0"], tok_pb["SA1"], tok_sc_rd[0]],
                sb_free=[tok_pb["SB0"], tok_pb["SB1"], tok_sc_rd[1]])
            tok_rows_free = rows_rd1
            tok_C_free = [t_y16s[-1]]
            tok_pb["SA0"] = tok_pb["SA1"] = t_sub1
            tok_pb["SB0"] = tok_pb["SB1"] = t_norm1[-1]


            t_g_free = [t_wo_free[0], t_wo_free[1]]
            t_u_free = [tok_vtm_free[0], tok_vtm_free[1]]
            t_H = []
            gms = []
            ums = []

            for hc in range(NHC):
                gb = hc % 2
                t_gl = sc.dma(
                    "g",
                    lambda g, hc=hc, gb=gb: g.dma_start(
                        AR[:, AR_GS[gb]:AR_GS[gb] + 2048],
                        wg[hc, :, :]),
                    deps=[t_g_free[gb]], inc="Dg%d" % gb)
                t_ul = sc.dma(
                    "g",
                    lambda g, hc=hc, gb=gb: g.dma_start(
                        AR[:, AR_US[gb]:AR_US[gb] + 2048],
                        wu[hc, :, :]),
                    deps=[t_u_free[gb]], inc="Du%d" % gb)
                if hc >= 2:
                    MM(SM[0:1, :], onesb[:, :],
                       AR[:, AR_H + (hc - 2) * 512:AR_H + (hc - 1) * 512],
                       hc - 2 == 0, False,
                       [t_H[hc - 2]] + ([tok_pb["SM"]] if hc == 2 else []))
                gn = "SA0" if hc % 2 == 0 else "SB0"
                un = "SA1" if hc % 2 == 0 else "SB1"
                pgb = SA0 if hc % 2 == 0 else SB0
                pub = SA1 if hc % 2 == 0 else SB1
                gsil_o = AR_GSIL + (hc % 2) * 512
                for dc in range(NEC):
                    d = [t_gl, tok_pb[gn]] if dc == 0 else []
                    if dc % 4 == 0:
                        d = d + [t_y16s[dc // 4]]
                    t_gm = MM(pgb,
                              AR[:, AR_GS[gb] + dc * 128:AR_GS[gb] + (dc + 1) * 128],
                              C[:, dc * 512:(dc + 1) * 512],
                              dc == 0, dc == NEC - 1, d,
                              inc="T" if dc == NEC - 1 else None)
                for dc in range(NEC):
                    d = [t_ul, tok_pb[un]] if dc == 0 else []
                    if dc % 4 == 0:
                        d = d + [t_y16s[dc // 4]]
                    t_um = MM(pub,
                              AR[:, AR_US[gb] + dc * 128:AR_US[gb] + (dc + 1) * 128],
                              C[:, dc * 512:(dc + 1) * 512],
                              dc == 0, dc == NEC - 1, d,
                              inc="T" if dc == NEC - 1 else None)
                gms.append(t_gm)
                ums.append(t_um)
                t_g_free[gb] = t_gm
                t_u_free[gb] = t_um
                t_sil = sc.st(
                    "s",
                    lambda s, hc=hc, pgb=pgb, gsil_o=gsil_o: s.activation(
                        AR[:, gsil_o:gsil_o + 512], pgb, AF.Silu,
                        scale=sgT[:, hc:hc + 1]),
                    deps=[t_gm, t_hc_last if hc <= 1 else None,
                          t_H[hc - 2] if hc >= 2 else None], inc="S")
                tok_pb[gn] = t_sil
                t_hcv = sc.st(
                    "v",
                    lambda v, hc=hc, pub=pub, gsil_o=gsil_o:
                    v.scalar_tensor_tensor(
                        AR[:, AR_H + hc * 512:AR_H + (hc + 1) * 512],
                        pub, suT[:, hc:hc + 1],
                        AR[:, gsil_o:gsil_o + 512], ALU.mult, ALU.mult),
                    deps=[t_um, t_sil], inc="V")
                tok_pb[un] = t_hcv
                t_hc_last = t_hcv
                t_H.append(t_hcv)

            MM(SM[0:1, :], onesb[:, :],
               AR[:, AR_H + 42 * 512:AR_H + 43 * 512], False, False,
               [t_H[42]])
            t_cs = MM(SM[0:1, :], onesb[:, :],
                      AR[:, AR_H + 43 * 512:AR_H + 44 * 512], False, True,
                      [t_H[43]], inc="T")
            t_rowb = sc.st(
                "s", lambda s: s.copy(rowb[0:1, :], SM[0:1, :]),
                deps=[t_cs, tok_rowb_free], inc="S")
            tok_pb["SM"] = t_rowb

            t_d_free = [[], []]
            pc_dep = [[gms[42]], [gms[43]], [gms[43]], [ums[42]]]
            pc_span = [(0, 2048), (2048, 2816), (2816, 4096), (4096, 5632)]
            pc_sem = ("Dd0", "Dd1", "Dd2", "Dd3")
            t_z = []
            xw_n = []
            for ec in range(NEC):
                dn = "O0" if ec % 2 == 0 else "O1"
                pbd = O0 if ec % 2 == 0 else O1
                hw = [None] * 4
                for p in range(4):
                    c0, c1 = pc_span[p]
                    hw[p] = sc.dma(
                        "g",
                        lambda g, ec=ec, c0=c0, c1=c1: g.dma_start(
                            AR[:, c0:c1], wd[ec, :, c0:c1]),
                        deps=pc_dep[p] + [t_d_free[p // 2]],
                        inc=pc_sem[p])
                if ti + 1 < NT and ec < 4:
                    q0n = (ti + 1) * TT
                    xw_n.append(sc.dma(
                        "g",
                        lambda g, q0n=q0n, k=ec: g.dma_start(
                            C[:, k * 2048:(k + 1) * 2048],
                            bass.AP(x_in, q0n + 4 * k * 128 * 2048,
                                    [[2048, 128], [128 * 2048, 4],
                                     [1, 512]])),
                        deps=[t_u_free[1]]
                             + ([xw_n[ec - 2]] if ec >= 2 else []),
                        inc="Dx%d" % (ec % 2)))
                t_dm = None
                for hc in range(NHC):
                    half = 0 if hc < 22 else 1
                    off = AR_DS[half] + (hc - 22 * half) * 128
                    d = [t_H[hc]]
                    if hc == 0:
                        d += [hw[0], tok_pb[dn]]
                    elif hc == 16:
                        d += [hw[1]]
                    elif hc == 22:
                        d += [hw[2]]
                    elif hc == 32:
                        d += [hw[3]]
                    t_dm = MM(pbd[:, :], AR[:, off:off + 128],
                              AR[:, AR_H + hc * 512:AR_H + (hc + 1) * 512],
                              hc == 0, False, d,
                              inc="T" if hc == 21 else None)
                    if hc == 21:
                        t_d_free[0] = t_dm
                t_dm = MM(pbd[:, :], m128b[0:1, :], rowb[0:1, :],
                          False, True, [t_rowb], inc="T")
                tok_rowb_free = t_dm
                t_d_free[1] = t_dm
                d_z = [t_dm]
                t_zc = sc.st(
                    "v",
                    lambda v, ec=ec, pbd=pbd: v.scalar_tensor_tensor(
                        B[:, ec * 512:(ec + 1) * 512], pbd[:, :],
                        sdT[:, ec:ec + 1], A[:, ec * 512:(ec + 1) * 512],
                        ALU.mult, ALU.add),
                    deps=d_z, inc="V")
                tok_pb[dn] = t_zc
                t_z.append(t_zc)

            if ti + 1 < NT:
                hzf = [t_d_free[1]]
                scrf = hzf
                tok_kh_free = [t_d_free[1], t_d_free[1]]
                tok_vtm_free = [t_d_free[1], t_d_free[1]]
                emit_astart(ti + 1, xw_n, hzf, scrf)

            t_out, _, t_sq2, rows_rd2, t_st2_ln2, t_sub2 = layernorm(
                B, A, t_z, AR_CAT, [t_z[-1]], cast_out=False,
                sa_free=[tok_pb["O0"]], sb_free=[tok_pb["O1"]],
                out_in_src=True, f32_stats=True, wide=False)
            tok_rows_free = rows_rd2
            tok_pb["O0"] = tok_pb["O1"] = t_out[-1]

            t_st = sc.dma(
                "s",
                lambda s, q0=q0: s.dma_start(
                    bass.AP(y_out, q0,
                            [[2048, 128], [128 * 2048, 16], [1, 512]]),
                    B[:, :]),
                deps=t_out, inc="Dy")

            tok_A_free = list(t_out)
            tok_B_free = [t_st]
            tok_sbinv_free = t_d_free[1]
            tok_cat_free = [t_st2_ln2] * NH

        sc.st("q", lambda q: q.nop(),
              deps=[("Dy", sc.counts.get("Dy", 0))])

        with nc.Block() as block:
            @block.tensor
            def _(t):
                sc.replay("t", t, sems)

            @block.vector
            def _(v):
                sc.replay("v", v, sems)

            @block.scalar
            def _(s):
                sc.replay("s", s, sems)

            @block.gpsimd
            def _(g):
                sc.replay("g", g, sems)

            @block.sync
            def _(q):
                sc.replay("q", q, sems)

    return nc


def _host_layouts(inputs):
    perm = np.array(list(range(0, 64, 2)) + list(range(1, 64, 2)))

    def blk_e(w):
        F = w.shape[1]
        return np.ascontiguousarray(
            w.reshape(16, 128, F).transpose(1, 0, 2).reshape(128, 16 * F))

    def blk_h(w):
        return np.ascontiguousarray(
            w.transpose(1, 0, 2).reshape(128, w.shape[0] * w.shape[2]))

    perm_swap = np.concatenate([perm[32:], perm[:32]])
    zc = np.zeros((E, 64), np.float32)
    zh = np.zeros((NH, 128, 64), np.float32)

    d = {}
    d["wdkv"] = blk_e(np.asarray(inputs["L0_attn_W_dkv"], np.float32))
    d["wdq"] = blk_e(np.asarray(inputs["L0_attn_W_dq"], np.float32))
    wkr = np.asarray(inputs["L0_attn_W_kr"], np.float32)
    d["wkrp"] = blk_e(np.concatenate([zc, wkr[:, perm]], 1))
    d["wkrs"] = blk_e(np.concatenate([zc, wkr[:, perm_swap]], 1))
    wqc = np.asarray(inputs["L0_attn_W_qc"], np.float32)
    wqr = np.asarray(inputs["L0_attn_W_qr"], np.float32)
    d["wqcr"] = blk_h(np.concatenate([wqc, wqr[:, :, perm]], 2))
    d["wqsw"] = blk_h(np.concatenate([zh, wqr[:, :, perm_swap]], 2))
    d["wuk"] = blk_h(np.asarray(inputs["L0_attn_W_uk"], np.float32))
    d["wuv"] = blk_h(np.asarray(inputs["L0_attn_W_uv"], np.float32))
    wo = np.asarray(inputs["L0_attn_W_o"], np.float32)
    d["wo"] = np.ascontiguousarray(
        wo.reshape(16, 128, 16, 128).transpose(2, 1, 0, 3).reshape(16, 128, 2048))
    wgq = np.asarray(inputs["L0_ffn_wq_gate"], np.int32).astype(np.uint8).T
    d["wg"] = np.ascontiguousarray(
        wgq.reshape(16, 128, 44, 128).transpose(2, 1, 0, 3).reshape(44, 128, 2048))
    wuq = np.asarray(inputs["L0_ffn_wq_up"], np.int32).astype(np.uint8).T
    d["wu"] = np.ascontiguousarray(
        wuq.reshape(16, 128, 44, 128).transpose(2, 1, 0, 3).reshape(44, 128, 2048))
    wdn = np.asarray(inputs["L0_ffn_wq_down"], np.int32).astype(np.uint8).T
    d["wd"] = np.ascontiguousarray(
        wdn.reshape(44, 128, 16, 128).transpose(2, 1, 0, 3).reshape(16, 128, 5632))
    d["sg"] = np.ascontiguousarray(
        np.asarray(inputs["L0_ffn_s_gate"], np.float32).reshape(44, 128).T)
    d["su"] = np.ascontiguousarray(
        np.asarray(inputs["L0_ffn_s_up"], np.float32).reshape(44, 128).T)
    d["sd"] = np.ascontiguousarray(
        np.asarray(inputs["L0_ffn_s_down"], np.float32).reshape(16, 128).T)

    half = DR // 2
    freqs = np.exp(np.arange(half, dtype=np.float64) *
                   (-np.log(10000.0) / half))
    ang = np.arange(S, dtype=np.float64)[:, None] * freqs[None, :]
    cos = np.cos(ang).T.astype(np.float32)
    sin = np.sin(ang).T.astype(np.float32)
    rt = np.zeros((128, 2 * S), np.float32)
    rt[64:96, 0:S] = cos
    rt[96:128, 0:S] = cos
    rt[64:96, S:2 * S] = -sin
    rt[96:128, S:2 * S] = sin
    d["rtab"] = rt

    p = np.arange(128)[:, None]
    j = np.arange(896)[None, :]
    d["mks"] = np.where(p <= j - 384, 0.0, -1e30).astype(np.float32)
    d["idm"] = np.eye(128, dtype=np.float32)
    return d


def kernel(L0_attn_x, **weights):
    weights["L0_attn_x"] = L0_attn_x
    shared = _host_layouts(weights)
    x = np.asarray(L0_attn_x, np.float32)
    in_maps = []
    for b in range(8):
        m = dict(shared)
        m["x"] = np.ascontiguousarray(x[b].T)
        in_maps.append(m)
    nc = bass.Bass("TRN2", target_bir_lowering=False)
    build_program(nc)
    res = run_bass_kernel_spmd(nc, in_maps, core_ids=list(range(8)))
    out = np.stack([np.asarray(r["y"]).T for r in res.results], 0)
    return out.astype(np.float32)


if __name__ == "__main__":
    nc_ = bass.Bass("TRN2", target_bir_lowering=False)
    build_program(nc_)
    try:
        n = len(list(nc_.all_instructions()))
    except Exception:
        n = -1
    print("program built, instructions:", n)


# version 41
# speedup vs baseline: 1.0495x; 1.0495x over previous
import numpy as np
import concourse.bass as bass
import concourse.mybir as mybir
from concourse.bass_utils import run_bass_kernel_spmd

AF = mybir.ActivationFunctionType
ALU = mybir.AluOpType
F32 = mybir.dt.float32
BF16 = mybir.dt.bfloat16
I32 = mybir.dt.int32
U8 = mybir.dt.uint8

S = 2048
E = 2048
NH = 16
DR = 64
DH = 128
HID = 5632
NHC = HID // 128
NEC = E // 128
TT = 512
NT = S // TT
QSCALE = float(1.0 / np.sqrt(DH))

ENGINES = ("t", "v", "s", "g", "q")


class Sched:

    def __init__(self):
        self.steps = {e: [] for e in ENGINES}
        self.counts = {}

    def st(self, eng, fn, deps=(), inc=None, amt=1):
        waits = []
        for d in deps:
            if d is None:
                continue
            if isinstance(d, list):
                waits += [x for x in d if x is not None]
            else:
                waits.append(d)
        tok = None
        if inc is not None:
            self.counts[inc] = self.counts.get(inc, 0) + amt
            tok = (inc, self.counts[inc])
        self.steps[eng].append((fn, waits, inc, amt))
        return tok

    def dma(self, eng, fn, deps=(), inc=None):
        return self.st(eng, fn, deps, inc, 16)

    def replay(self, eng_name, eng, sems):
        wm = {}
        for fn, waits, inc, amt in self.steps[eng_name]:
            for (sname, val) in waits:
                if wm.get(sname, 0) < val:
                    eng.wait_ge(sems[sname], val)
                    wm[sname] = val
            inst = fn(eng)
            if inc is not None:
                inst.then_inc(sems[inc], amt)


AR_N = 31744
AR_KH = (0, 2048)
AR_VT = (4096, 6144)
AR_ET = (8192, 9216)
AR_XDQ = 8192
AR_QA = 10240
AR_CAT = 18432
AR_SBI = 26624
AR_WO = (0, 2048)
AR_Y2 = 4096
AR_GS = (0, 2048)
AR_US = (4096, 6144)
AR_H = 8192
AR_DS = (0, 2816)
AR_GSIL = 30720


def build_program(nc):
    x_in = nc.dram_tensor("x", [E, S], F32, kind="ExternalInput")
    wdkv = nc.dram_tensor("wdkv", [128, NEC * 128], F32, kind="ExternalInput")
    wdq = nc.dram_tensor("wdq", [128, NEC * 128], F32, kind="ExternalInput")
    wkrp = nc.dram_tensor("wkrp", [128, NEC * 128], F32, kind="ExternalInput")
    wkrs = nc.dram_tensor("wkrs", [128, NEC * 128], F32, kind="ExternalInput")
    wqcr = nc.dram_tensor("wqcr", [128, NH * 128], F32, kind="ExternalInput")
    wqsw = nc.dram_tensor("wqsw", [128, NH * 128], F32, kind="ExternalInput")
    wuk = nc.dram_tensor("wuk", [128, NH * 64], F32, kind="ExternalInput")
    wuv = nc.dram_tensor("wuv", [128, NH * 128], F32, kind="ExternalInput")
    rtab = nc.dram_tensor("rtab", [128, 2 * S], F32, kind="ExternalInput")
    wo = nc.dram_tensor("wo", [NEC, 128, NEC * 128], F32, kind="ExternalInput")
    wg = nc.dram_tensor("wg", [NHC, 128, NEC * 128], U8, kind="ExternalInput")
    wu = nc.dram_tensor("wu", [NHC, 128, NEC * 128], U8, kind="ExternalInput")
    wd = nc.dram_tensor("wd", [NEC, 128, NHC * 128], U8, kind="ExternalInput")
    sg_in = nc.dram_tensor("sg", [128, NHC], F32, kind="ExternalInput")
    su_in = nc.dram_tensor("su", [128, NHC], F32, kind="ExternalInput")
    sd_in = nc.dram_tensor("sd", [128, NEC], F32, kind="ExternalInput")
    mks = nc.dram_tensor("mks", [128, 896], F32, kind="ExternalInput")
    idm = nc.dram_tensor("idm", [128, 128], F32, kind="ExternalInput")
    y_out = nc.dram_tensor("y", [E, S], F32, kind="ExternalOutput")

    wo_b = nc.dram_tensor("wo_b", [NEC, 128, NEC * 128], BF16, kind="Internal")

    sc = Sched()

    from contextlib import ExitStack
    with ExitStack() as ctx:
        def sb(name, shape, dt):
            return ctx.enter_context(nc.sbuf_tensor(name, shape, dt))

        def ps(name):
            return ctx.enter_context(nc.psum_tensor(name, [128, 512], F32))

        A = sb("A", [128, 8192], F32)
        B = sb("B", [128, 8192], F32)
        C = sb("C", [128, 8192], BF16)
        AR = sb("AR", [128, AR_N], BF16)
        ckv = sb("ckv", [128, S], BF16)
        krc = sb("krc", [128, S], BF16)
        rtabb = sb("rtabb", [128, 2 * S], BF16)
        mkb = sb("mkb", [128, 896], BF16)
        idb = sb("idb", [128, 128], BF16)
        w_dkv = sb("w_dkv", [128, NEC * 128], BF16)
        w_dq = sb("w_dq", [128, NEC * 128], BF16)
        w_krp = sb("w_krp", [128, NEC * 128], BF16)
        w_krs = sb("w_krs", [128, NEC * 128], BF16)
        w_qcr = sb("w_qcr", [128, NH * 128], BF16)
        w_qsw = sb("w_qsw", [128, NH * 128], BF16)
        w_uk = sb("w_uk", [128, NH * 64], BF16)
        w_uv = sb("w_uv", [128, NH * 128], BF16)
        sgT = sb("sgT", [128, NHC], F32)
        suT = sb("suT", [128, NHC], F32)
        sdT = sb("sdT", [128, NEC], F32)
        rows = sb("rows", [1, 1536], F32)
        onesb = sb("onesb", [128, 1], BF16)
        onesf = sb("onesf", [1, 128], F32)
        onesf2 = sb("onesf2", [128, 1], F32)
        m128b = sb("m128b", [1, 128], BF16)
        rowb = sb("rowb", [1, 512], BF16)
        SA = ctx.enter_context(nc.psum_tensor("SA", [128, 1024], F32))
        SB = ctx.enter_context(nc.psum_tensor("SB", [128, 1024], F32))
        O0, O1, SM, KV = ps("O0"), ps("O1"), ps("SM"), ps("KV")
        sem_names = ["T", "V", "S", "G", "Dx0", "Dx1", "Dx2", "Dx3", "Dsw0", "Dsw1", "Dsc",
                     "Dst0", "Dst1",
                     "Dwo0", "Dwo1", "Dg0", "Dg1", "Du0", "Du1",
                     "Dd0", "Dd1", "Dd2", "Dd3", "Dy"]
        sems = {n: ctx.enter_context(nc.semaphore("sem" + n))
                for n in sem_names}
        SA0, SA1 = SA[:, 0:512], SA[:, 512:1024]
        SB0, SB1 = SB[:, 0:512], SB[:, 512:1024]
        sbinv = AR[:, AR_SBI:AR_SBI + 1024].bitcast(F32)

        def MM(psum_ap, lhsT_ap, rhs_ap, start, stop, deps, inc=None):
            return sc.st(
                "t",
                lambda t, o=psum_ap, l=lhsT_ap, r=rhs_ap, a=start, b=stop:
                    t.matmul(o, l, r, start=a, stop=b),
                deps=deps, inc=inc)

        t_ones_b = sc.st("v", lambda v: v.memset(onesb[:], 1.0), inc="V")
        t_ones_f = sc.st("v", lambda v: v.memset(onesf[:], 1.0), inc="V")
        t_ones_f2 = sc.st("v", lambda v: v.memset(onesf2[:], 1.0), inc="V")
        t_m128 = sc.st("v", lambda v: v.memset(m128b[:], -128.0), inc="V")
        sc.dma("q", lambda q: q.dma_start(sgT[:], sg_in.ap()), inc="Dsc")
        sc.dma("q", lambda q: q.dma_start(suT[:], su_in.ap()), inc="Dsc")
        t_sd = ("Dsc", 48)
        sc.dma("q", lambda q: q.dma_start(sdT[:], sd_in.ap()), inc="Dsc")

        small_w = [
            (rtab, rtabb[:, :], 4096), (mks, mkb[:, :], 896),
            (idm, idb[:, :], 128),
            (wdkv, w_dkv[:, :], 2048), (wdq, w_dq[:, :], 2048),
            (wkrp, w_krp[:, :], 2048), (wkrs, w_krs[:, :], 2048),
            (wqcr, w_qcr[:, :], 2048), (wqsw, w_qsw[:, :], 2048),
            (wuk, w_uk[:, :], 1024), (wuv, w_uv[:, :], 2048),
        ]
        half_free = [None, None]
        for i, (src, dst, n) in enumerate(small_w):
            hb = i % 2
            off = hb * 4096
            t_ld = sc.dma(
                "q",
                lambda q, src=src, off=off, n=n: q.dma_start(
                    B[:, off:off + n], src.ap()),
                deps=[half_free[hb]], inc="Dsw%d" % hb)
            t_cv = sc.st(
                "v",
                lambda v, dst=dst, off=off, n=n: v.tensor_copy(
                    dst, B[:, off:off + n]),
                deps=[t_ld], inc="V")
            half_free[hb] = t_cv

        store_tok = {}
        for ec in range(NEC):
            hb = ec % 2
            off = hb * 4096
            t_ld = sc.dma(
                "q",
                lambda q, ec=ec, off=off: q.dma_start(
                    B[:, off:off + 2048], wo[ec, :, :]),
                deps=[half_free[hb]], inc="Dsw%d" % hb)
            cto = AR_CAT + hb * 2048
            t_cv = sc.st(
                "s",
                lambda s, off=off, cto=cto: s.activation(
                    AR[:, cto:cto + 2048], B[:, off:off + 2048],
                    AF.Copy),
                deps=[t_ld], inc="S")
            t_stb = sc.dma(
                "s",
                lambda s, ec=ec, cto=cto: s.dma_start(
                    wo_b[ec, :, :], AR[:, cto:cto + 2048]),
                deps=[t_cv], inc="Dst%d" % hb)
            half_free[hb] = t_stb
            store_tok.setdefault(("wo_b", ec), []).append(t_stb)


        prep_done = [t for t in half_free if t is not None] + [
            t_sd, t_ones_b, t_ones_f, t_ones_f2, t_m128,
            ("Dst0", sc.counts.get("Dst0", 0)),
            ("Dst1", sc.counts.get("Dst1", 0))]

        tok_A_free = list(prep_done)
        tok_B_free = list(prep_done)
        tok_C_free = list(prep_done)
        tok_pb = {k: None for k in
                  ("SA0", "SA1", "SB0", "SB1", "O0", "O1", "SM", "KV")}
        tok_cat_free = [[("Dst0", sc.counts.get("Dst0", 0)),
                         ("Dst1", sc.counts.get("Dst1", 0))]] * NH
        tok_kh_free = [None, None]
        tok_vtm_free = [None, None]
        tok_qall_free = None
        tok_rows_free = []
        tok_sbinv_free = None
        tok_sc_rd = [None, None]
        tok_expt_rd = [None, None]
        tok_rowb_free = None
        t_hc_last = None

        kv_tok = [[], []]
        carry = {}
        scrA = AR[64:128, 0:512]
        scrB = AR[64:128, 512:1024]
        kh_waits_init = [None]

        def make_kv_items(hh, ctx2, cache_w, q_ev):
            par = hh % 2
            khs = AR_KH[par]
            vts = AR_VT[par]
            nc52 = ctx2 // 512
            items = []

            def k_piece(c5, hh=hh, par=par, khs=khs):
                t_kn = MM(KV[0:64, :], w_uk[:, hh * 64:(hh + 1) * 64],
                          ckv[:, c5 * 512:(c5 + 1) * 512], True, True,
                          [tok_pb["KV"]] + (cache_w + q_ev
                                            if hh <= 1 and c5 == 0 else []),
                          inc="T")
                t_kn_e = sc.st(
                    "s",
                    lambda s, khs=khs, c5=c5: s.copy(
                        AR[0:64, khs + c5 * 512:khs + (c5 + 1) * 512],
                        KV[0:64, :]),
                    deps=[t_kn, tok_kh_free[par]], inc="S")
                tok_pb["KV"] = t_kn_e
                kv_tok[par].append(t_kn_e)

            def kr_piece(hh=hh, par=par, khs=khs, ctx2=ctx2):
                t_krcp = sc.st(
                    "v",
                    lambda v, khs=khs, ctx2=ctx2: v.tensor_copy(
                        AR[64:128, khs:khs + ctx2], krc[64:128, 0:ctx2]),
                    deps=[tok_kh_free[par]] + cache_w
                         + ([q_ev[-1]] if hh <= 1 else []), inc="V")
                kv_tok[par].append(t_krcp)

            def v_piece(tg, hh=hh, par=par, vts=vts):
                t_v = None
                for i in range(4):
                    tc = tg * 4 + i
                    t_v = MM(KV[:, i * 128:(i + 1) * 128],
                             ckv[:, tc * 128:(tc + 1) * 128],
                             w_uv[:, hh * 128:(hh + 1) * 128],
                             True, True,
                             [tok_pb["KV"]] if i == 0 else [],
                             inc="T" if i == 3 else None)
                t_v_e = sc.st(
                    "v",
                    lambda v, vts=vts, tg=tg: v.tensor_copy(
                        AR[:, vts + tg * 512:vts + (tg + 1) * 512],
                        KV[:, :]),
                    deps=[t_v, tok_vtm_free[par]], inc="V")
                tok_pb["KV"] = t_v_e
                kv_tok[par].append(t_v_e)

            for c5 in range(nc52):
                items.append(lambda c5=c5: k_piece(c5))
            items.append(kr_piece)
            for tg in range(nc52):
                items.append(lambda tg=tg: v_piece(tg))
            return items

        def emit_xloads(ti2, base_deps):
            q02 = ti2 * TT
            toks = []
            for k in range(4):
                toks.append(sc.dma(
                    "g",
                    lambda g, q02=q02, k=k: g.dma_start(
                        C[:, k * 2048:(k + 1) * 2048],
                        bass.AP(x_in, q02 + 4 * k * 128 * 2048,
                                [[2048, 128], [128 * 2048, 4], [1, 512]])),
                    deps=base_deps + ([toks[k - 2]] if k >= 2 else []),
                    inc="Dx%d" % (k % 2)))
            return toks

        def emit_astart(ti2, xw, dep_hz, dep_scr):
            q02 = ti2 * TT
            for kc in range(NEC):
                d = [tok_pb["SA0"]] if kc == 0 else []
                d += [xw[kc // 4]] if kc % 4 == 0 else []
                t_ckv = MM(SA0, w_dkv[:, kc * 128:(kc + 1) * 128],
                           C[:, kc * 512:(kc + 1) * 512],
                           kc == 0, kc == NEC - 1, d,
                           inc="T" if kc == NEC - 1 else None)
            for kc in range(NEC):
                d = [tok_pb["SA1"]] if kc == 0 else []
                t_xdq = MM(SA1, w_dq[:, kc * 128:(kc + 1) * 128],
                           C[:, kc * 512:(kc + 1) * 512],
                           kc == 0, kc == NEC - 1, d,
                           inc="T" if kc == NEC - 1 else None)
            for kc in range(NEC):
                d = [tok_pb["SB0"]] if kc == 0 else []
                t_krp = MM(SB0, w_krp[:, kc * 128:(kc + 1) * 128],
                           C[:, kc * 512:(kc + 1) * 512],
                           kc == 0, kc == NEC - 1, d,
                           inc="T" if kc == NEC - 1 else None)
            for kc in range(NEC):
                d = [tok_pb["SB1"]] if kc == 0 else []
                t_krs = MM(SB1, w_krs[:, kc * 128:(kc + 1) * 128],
                           C[:, kc * 512:(kc + 1) * 512],
                           kc == 0, kc == NEC - 1, d,
                           inc="T" if kc == NEC - 1 else None)
            c_last = t_krs

            t_ckv_e = sc.st(
                "s",
                lambda s, q02=q02: s.copy(ckv[:, q02:q02 + 512], SA0),
                deps=[t_ckv], inc="S")
            tok_pb["SA0"] = t_ckv_e
            t_xdq_e = sc.st(
                "s",
                lambda s: s.copy(AR[:, AR_XDQ:AR_XDQ + 512], SA1),
                deps=[t_xdq, tok_qall_free, tok_sc_rd[0]] + dep_hz, inc="S")
            tok_pb["SA1"] = t_xdq_e

            ccl = rtabb[64:128, q02:q02 + 512]
            ssl = rtabb[64:128, S + q02:S + q02 + 512]
            t_k1 = sc.st(
                "v",
                lambda v, ccl=ccl: v.tensor_tensor(
                    scrA, SB[64:128, 0:512], ccl, ALU.mult),
                deps=[t_krp] + dep_scr, inc="V")
            t_k2 = sc.st(
                "v",
                lambda v, ssl=ssl: v.tensor_tensor(
                    scrB, SB[64:128, 512:1024], ssl, ALU.mult),
                deps=[t_krs], inc="V")
            t_k3 = sc.st(
                "v",
                lambda v, q02=q02: v.tensor_tensor(
                    krc[64:128, q02:q02 + 512], scrA, scrB, ALU.add),
                deps=[t_k1, t_k2], inc="V")
            tok_pb["SB0"] = t_k1
            tok_pb["SB1"] = t_k2
            cache_w = [t_ckv_e, t_k3]

            t_q_ev = []
            pb_rd = {"O0": [tok_pb["O0"]], "O1": [tok_pb["O1"]],
                     "SM": [tok_pb["SM"]], "KV": [tok_pb["KV"]]}
            prev_add = None
            for h in range(NH):
                na = "O0" if h % 2 == 0 else "O1"
                nb = "SM" if h % 2 == 0 else "KV"
                Pa = O0 if h % 2 == 0 else O1
                Pb = SM if h % 2 == 0 else KV
                t_q1 = MM(Pa[:, :], w_qcr[:, h * 128:(h + 1) * 128],
                          AR[:, AR_XDQ:AR_XDQ + 512], True, True,
                          [t_xdq_e] + pb_rd[na], inc="T")
                t_q2 = MM(Pb[:, :], w_qsw[:, h * 128:(h + 1) * 128],
                          AR[:, AR_XDQ:AR_XDQ + 512], True, True,
                          pb_rd[nb], inc="T")
                t_qn_e = sc.st(
                    "s",
                    lambda s, h=h, b=Pa: s.copy(
                        AR[0:64, AR_QA + h * 512:AR_QA + h * 512 + 512],
                        b[0:64, :]),
                    deps=[t_q1, tok_qall_free], inc="S")
                t_m1 = sc.st(
                    "v",
                    lambda v, ccl=ccl, b=Pa: v.tensor_tensor(
                        scrA, b[64:128, :], ccl, ALU.mult),
                    deps=[t_q1, prev_add, t_k3 if h == 0 else None],
                    inc="V")
                t_m2 = sc.st(
                    "v",
                    lambda v, ssl=ssl, b=Pb: v.tensor_tensor(
                        scrB, b[64:128, :], ssl, ALU.mult),
                    deps=[t_q2], inc="V")
                t_add = sc.st(
                    "v",
                    lambda v, h=h: v.tensor_tensor(
                        AR[64:128, AR_QA + h * 512:AR_QA + h * 512 + 512],
                        scrA, scrB, ALU.add),
                    deps=[t_m1, t_m2, tok_qall_free], inc="V")
                prev_add = t_add
                pb_rd[na] = [t_qn_e, t_m1]
                pb_rd[nb] = [t_m2]
                t_q_ev += [t_qn_e, t_add]
            for k2 in ("O0", "O1", "SM", "KV"):
                tok_pb[k2] = pb_rd[k2]

            ctx2 = (ti2 + 1) * TT
            for hh in (0, 1):
                for it in make_kv_items(hh, ctx2, cache_w, t_q_ev):
                    it()
            carry["ast"] = {"q_ev": t_q_ev, "cache_w": cache_w,
                            "ccl": ccl, "ssl": ssl, "c_last": c_last}

        for ti in range(NT):
            q0 = ti * TT
            ctx = (ti + 1) * TT
            nc5 = ctx // 512
            ntc = ctx // 128
            npair = ntc // 2

            if ti == 0:
                xw0 = emit_xloads(0, list(prep_done))
                emit_astart(0, xw0, [tok_qall_free], kh_waits_init)
            ast = carry.pop("ast")
            t_q_ev = ast["q_ev"]
            tok_cache_w = ast["cache_w"]
            ccl = ast["ccl"]
            ssl = ast["ssl"]
            tok_C_free = [ast["c_last"]]

            t_xr = []
            for j in range(2):
                t_xr.append(sc.dma(
                    "q",
                    lambda q, q0=q0, j=j: q.dma_start(
                        A[:, j * 4096:(j + 1) * 4096],
                        bass.AP(x_in, q0 + 8 * j * 128 * 2048,
                                [[2048, 128], [128 * 2048, 8], [1, 512]])),
                    deps=tok_A_free, inc="Dx%d" % (2 + j)))

            pending_tail = None
            pending_tail_b = None
            st8 = {"rows": tok_rows_free, "sbinv": tok_sbinv_free,
                   "si": None}
            for h in range(NH):
                kb = h % 2
                oname = "O0" if kb == 0 else "O1"
                O = O0 if kb == 0 else O1
                khs = AR_KH[kb]
                vts = AR_VT[kb]
                pending = (make_kv_items(h + 1, ctx, tok_cache_w, t_q_ev)
                           if 1 <= h < NH - 1 else [])
                first_deps = kv_tok[kb]
                kv_tok[kb] = []

                t_av = None
                t_sum = None
                t_exp = [None, None]
                for tp in range(npair + 1):
                    if tp < npair:
                        sb = tp % 2
                        SC = SA if sb == 0 else SB
                        etb = AR_ET[sb]
                        for i in range(2):
                            tc = 2 * tp + i
                            d = [tok_sc_rd[sb]] + first_deps
                            first_deps = []
                            diag = tc >= 4 * ti
                            t_scm = MM(SC[:, i * 512:(i + 1) * 512],
                                       AR[:, khs + tc * 128:khs + (tc + 1) * 128],
                                       AR[:, AR_QA + h * 512:AR_QA + (h + 1) * 512],
                                       True, not diag, d,
                                       inc=None if diag else "T")
                            if diag:
                                kk = 384 - 128 * (tc - 4 * ti)
                                t_scm = MM(SC[:, i * 512:(i + 1) * 512],
                                           idb[:, :],
                                           mkb[:, kk:kk + 512],
                                           False, True, [], inc="T")
                        t_exp[sb] = sc.st(
                            "s",
                            lambda s, SC=SC, etb=etb: s.activation(
                                AR[:, etb:etb + 1024], SC[:, 0:1024], AF.Exp,
                                scale=np.float32(QSCALE)),
                            deps=[t_scm, tok_expt_rd[sb]], inc="S")
                        tok_sc_rd[sb] = t_exp[sb]
                        if tp == 0 and pending_tail is not None:
                            pending_tail()
                            pending_tail = None
                    pa = tp - 1
                    if pa >= 0:
                        sbp = pa % 2
                        etbp = AR_ET[sbp]
                        for i in range(2):
                            tc = 2 * pa + i
                            d_av = [t_exp[sbp]]
                            if tc == 0:
                                d_av += [tok_pb[oname]]
                            t_av = MM(O[:, :],
                                      AR[:, vts + tc * 128:vts + (tc + 1) * 128],
                                      AR[:, etbp + i * 512:etbp + (i + 1) * 512],
                                      tc == 0, tc == ntc - 1, d_av, inc="T")
                        if pa == 0 and pending_tail_b is not None:
                            pending_tail_b()
                            pending_tail_b = None
                        for i in range(2):
                            tc = 2 * pa + i
                            d_sm = [tok_pb["SM"]] if tc == 0 else []
                            t_sum = MM(SM[0:1, :], onesb[:, :],
                                       AR[:, etbp + i * 512:etbp + (i + 1) * 512],
                                       tc == 0, tc == ntc - 1, d_sm, inc="T")
                        tok_expt_rd[sbp] = t_sum
                    if pending:
                        nleft = npair + 1 - tp
                        npop = (len(pending) + nleft - 1) // nleft
                        for _ in range(npop):
                            if pending:
                                pending.pop(0)()
                while pending:
                    pending.pop(0)()
                tok_kh_free[kb] = t_scm
                tok_vtm_free[kb] = t_av

                def mk_tail_a(t_sum=t_sum):
                    def tail_a():
                        t_ln = sc.st(
                            "s",
                            lambda s: s.activation(rows[0:1, 512:1024],
                                                   SM[0:1, :], AF.Ln),
                            deps=[t_sum] + st8["rows"], inc="S")
                        t_si = sc.st(
                            "s",
                            lambda s: s.activation(rows[0:1, 1024:1536],
                                                   rows[0:1, 512:1024],
                                                   AF.Exp,
                                                   scale=np.float32(-1.0)),
                            deps=[t_ln], inc="S")
                        st8["si"] = t_si
                    return tail_a

                def mk_tail_b(h=h, O=O, oname=oname, t_av=t_av):
                    def tail_b():
                        t_bc = MM(SM[:, :], onesf[0:1, :],
                                  rows[0:1, 1024:1536],
                                  True, True, [st8["si"]], inc="T")
                        st8["rows"] = [t_bc]
                        t_sbe = sc.st(
                            "v", lambda v: v.tensor_copy(sbinv, SM[:, :]),
                            deps=[t_bc, st8["sbinv"]], inc="V")
                        tok_pb["SM"] = t_sbe
                        t_mu = sc.st(
                            "v",
                            lambda v: v.tensor_tensor(
                                AR[:, AR_CAT + h * 512:
                                   AR_CAT + (h + 1) * 512],
                                O[:, :], sbinv, ALU.mult),
                            deps=[t_av, t_sbe, tok_cat_free[h]], inc="V")
                        tok_pb[oname] = t_mu
                        st8["sbinv"] = t_mu
                        tok_cat_free[h] = t_mu
                    return tail_b

                pending_tail = mk_tail_a()
                pending_tail_b = mk_tail_b()

            pending_tail()
            pending_tail_b()
            pending_tail = None
            pending_tail_b = None
            tok_rows_free = st8["rows"]
            tok_sbinv_free = st8["sbinv"]
            tok_qall_free = t_scm

            t_wo_free = [None, None]
            t_w2 = []
            for ec in range(NEC):
                wb = ec % 2
                dd = [t_wo_free[wb], tok_kh_free[wb]]
                if ti == 0:
                    dd += store_tok[("wo_b", ec)]
                t_ld = sc.dma(
                    "q",
                    lambda q, ec=ec, wb=wb: q.dma_start(
                        AR[:, AR_WO[wb]:AR_WO[wb] + 2048], wo_b[ec, :, :]),
                    deps=dd, inc="Dwo%d" % wb)
                wn = "SA0" if ec % 2 == 0 else "SB0"
                pbw = SA0 if ec % 2 == 0 else SB0
                for k in range(NEC):
                    d = [t_ld, tok_cat_free[k]]
                    if k == 0:
                        d += [tok_pb[wn], tok_sc_rd[ec % 2]]
                    t_mm = MM(pbw,
                              AR[:, AR_WO[wb] + k * 128:AR_WO[wb] + (k + 1) * 128],
                              AR[:, AR_CAT + k * 512:AR_CAT + (k + 1) * 512],
                              k == 0, k == NEC - 1, d,
                              inc="T" if k == NEC - 1 else None)
                t_wo_free[wb] = t_mm
                t_add = sc.st(
                    "v",
                    lambda v, ec=ec, pbw=pbw: v.tensor_tensor(
                        B[:, ec * 512:(ec + 1) * 512], pbw,
                        A[:, ec * 512:(ec + 1) * 512], ALU.add),
                    deps=[t_mm, t_xr[ec // 8]]
                         + (tok_B_free if ec == 0 else []), inc="V")
                tok_pb[wn] = t_add
                t_w2.append(t_add)

            def layernorm(src_f32, dst_f32, t_src_ready, y2_off, y2_free,
                          cast_out, sa_free, sb_free,
                          out_in_src=False, f32_stats=False):
                t_cast = []
                t_sq = []
                for j in range(4):
                    dj = (t_src_ready[4 * j:4 * j + 4]
                          if len(t_src_ready) == 16 else t_src_ready)
                    if not f32_stats:
                        t_cast.append(sc.st(
                            "s",
                            lambda s, j=j: s.activation(
                                C[:, j * 2048:(j + 1) * 2048],
                                src_f32[:, j * 2048:(j + 1) * 2048], AF.Copy),
                            deps=dj + (tok_C_free if j == 0 else []),
                            inc="S"))
                    t_sq.append(sc.st(
                        "s",
                        lambda s, j=j: s.activation(
                            AR[:, y2_off + j * 2048:y2_off + (j + 1) * 2048],
                            src_f32[:, j * 2048:(j + 1) * 2048], AF.Square),
                        deps=dj + (y2_free if j == 0 else []), inc="S"))
                for kc in range(NEC):
                    if f32_stats:
                        d1 = ([t_src_ready[kc]] if len(t_src_ready) == 16
                              else list(t_src_ready) if kc == 0 else [])
                        if kc == 0:
                            d1 += [tok_pb["SM"]]
                        t_st1 = MM(SM[0:1, :], onesf2[:, :],
                                   src_f32[:, kc * 512:(kc + 1) * 512],
                                   kc == 0, kc == NEC - 1, d1,
                                   inc="T" if kc == NEC - 1 else None)
                        continue
                    d1 = [t_cast[kc // 4]] if kc % 4 == 0 else []
                    if kc == 0:
                        d1 += [tok_pb["SM"]]
                    t_st1 = MM(SM[0:1, :], onesb[:, :],
                               C[:, kc * 512:(kc + 1) * 512],
                               kc == 0, kc == NEC - 1, d1,
                               inc="T" if kc == NEC - 1 else None)
                for kc in range(NEC):
                    d2 = [t_sq[kc // 4]] if kc % 4 == 0 else []
                    if kc == 0:
                        d2 += [tok_pb["KV"]]
                    t_st2 = MM(KV[0:1, :], onesb[:, :],
                               AR[:, y2_off + kc * 512:y2_off + (kc + 1) * 512],
                               kc == 0, kc == NEC - 1, d2,
                               inc="T" if kc == NEC - 1 else None)
                t_mean = sc.st(
                    "v",
                    lambda v: v.tensor_scalar(rows[0:1, 0:512], SM[0:1, :],
                                              1.0 / E, None, ALU.mult),
                    deps=[t_st1] + tok_rows_free, inc="V")
                tok_pb["SM"] = t_mean
                t_ey2 = sc.st(
                    "v",
                    lambda v: v.tensor_scalar(rows[0:1, 512:1024],
                                              KV[0:1, :],
                                              1.0 / E, None, ALU.mult),
                    deps=[t_st2], inc="V")
                tok_pb["KV"] = t_ey2
                t_bcm = [MM(SA0, onesf[0:1, :], rows[0:1, 0:512],
                            True, True, [t_mean] + sa_free, inc="T"),
                         MM(SA1, onesf[0:1, :], rows[0:1, 0:512],
                            True, True, [], inc="T")]
                t_m2 = sc.st(
                    "s",
                    lambda s: s.activation(rows[0:1, 1024:1536],
                                           rows[0:1, 0:512], AF.Square),
                    deps=[t_mean], inc="S")
                t_var = sc.st(
                    "v",
                    lambda v: v.tensor_tensor(rows[0:1, 0:512],
                                              rows[0:1, 512:1024],
                                              rows[0:1, 1024:1536],
                                              ALU.subtract),
                    deps=[t_ey2, t_m2] + t_bcm, inc="V")
                t_lnv = sc.st(
                    "s",
                    lambda s: s.activation(rows[0:1, 512:1024],
                                           rows[0:1, 0:512], AF.Ln,
                                           bias=np.float32(1e-5)),
                    deps=[t_var], inc="S")
                t_rstd = sc.st(
                    "s",
                    lambda s: s.activation(rows[0:1, 1024:1536],
                                           rows[0:1, 512:1024], AF.Exp,
                                           scale=np.float32(-0.5)),
                    deps=[t_lnv], inc="S")
                t_bcr = [MM(SB0, onesf[0:1, :], rows[0:1, 1024:1536],
                            True, True, [t_rstd] + sb_free, inc="T"),
                         MM(SB1, onesf[0:1, :], rows[0:1, 1024:1536],
                            True, True, [], inc="T")]
                t_norm = []
                t_s1 = None
                out_t = src_f32 if out_in_src else dst_f32
                for ec2 in range(NEC):
                    sab = SA0 if ec2 % 2 == 0 else SA1
                    sbb = SB0 if ec2 % 2 == 0 else SB1
                    d = t_bcm + t_bcr if ec2 == 0 else []
                    t_s1 = sc.st(
                        "v",
                        lambda v, ec2=ec2, sab=sab: v.tensor_tensor(
                            dst_f32[:, ec2 * 512:(ec2 + 1) * 512],
                            src_f32[:, ec2 * 512:(ec2 + 1) * 512],
                            sab, ALU.subtract),
                        deps=d, inc="V")
                    t_s2 = sc.st(
                        "v",
                        lambda v, ec2=ec2, sbb=sbb, out_t=out_t: v.tensor_tensor(
                            out_t[:, ec2 * 512:(ec2 + 1) * 512],
                            dst_f32[:, ec2 * 512:(ec2 + 1) * 512],
                            sbb, ALU.mult),
                        deps=[t_s1], inc="V")
                    t_norm.append(t_s2)
                t_co = None
                if cast_out:
                    t_co = []
                    for j in range(4):
                        t_co.append(sc.st(
                            "s",
                            lambda s, j=j: s.activation(
                                C[:, j * 2048:(j + 1) * 2048],
                                dst_f32[:, j * 2048:(j + 1) * 2048],
                                AF.Copy),
                            deps=t_norm[4 * j:4 * j + 4], inc="S"))
                return t_norm, t_co, t_sq[-1], t_bcm + t_bcr, t_st2, t_s1

            t_norm1, t_y16s, t_sq1, rows_rd1, _, t_sub1 = layernorm(
                B, A, t_w2, AR_H, [], cast_out=True,
                sa_free=[tok_pb["SA0"], tok_pb["SA1"], tok_sc_rd[0]],
                sb_free=[tok_pb["SB0"], tok_pb["SB1"], tok_sc_rd[1]])
            tok_rows_free = rows_rd1
            tok_C_free = [t_y16s[-1]]
            tok_pb["SA0"] = tok_pb["SA1"] = t_sub1
            tok_pb["SB0"] = tok_pb["SB1"] = t_norm1[-1]


            t_g_free = [t_wo_free[0], t_wo_free[1]]
            t_u_free = [tok_vtm_free[0], tok_vtm_free[1]]
            t_H = []
            gms = []
            ums = []

            for hc in range(NHC):
                gb = hc % 2
                t_gl = sc.dma(
                    "g",
                    lambda g, hc=hc, gb=gb: g.dma_start(
                        AR[:, AR_GS[gb]:AR_GS[gb] + 2048],
                        wg[hc, :, :]),
                    deps=[t_g_free[gb]], inc="Dg%d" % gb)
                t_ul = sc.dma(
                    "g",
                    lambda g, hc=hc, gb=gb: g.dma_start(
                        AR[:, AR_US[gb]:AR_US[gb] + 2048],
                        wu[hc, :, :]),
                    deps=[t_u_free[gb]], inc="Du%d" % gb)
                if hc >= 2:
                    MM(SM[0:1, :], onesb[:, :],
                       AR[:, AR_H + (hc - 2) * 512:AR_H + (hc - 1) * 512],
                       hc - 2 == 0, False,
                       [t_H[hc - 2]] + ([tok_pb["SM"]] if hc == 2 else []))
                if hc <= 1:
                    gn, un, pgb, pub = "O0", "O1", O0[:, :], O1[:, :]
                else:
                    gn = "SA0" if hc % 2 == 0 else "SB0"
                    un = "SA1" if hc % 2 == 0 else "SB1"
                    pgb = SA0 if hc % 2 == 0 else SB0
                    pub = SA1 if hc % 2 == 0 else SB1
                gsil_o = AR_GSIL + (hc % 2) * 512
                for dc in range(NEC):
                    d = [t_gl, tok_pb[gn]] if dc == 0 else []
                    if dc % 4 == 0:
                        d = d + [t_y16s[dc // 4]]
                    t_gm = MM(pgb,
                              AR[:, AR_GS[gb] + dc * 128:AR_GS[gb] + (dc + 1) * 128],
                              C[:, dc * 512:(dc + 1) * 512],
                              dc == 0, dc == NEC - 1, d,
                              inc="T" if dc == NEC - 1 else None)
                for dc in range(NEC):
                    d = [t_ul, tok_pb[un]] if dc == 0 else []
                    if dc % 4 == 0:
                        d = d + [t_y16s[dc // 4]]
                    t_um = MM(pub,
                              AR[:, AR_US[gb] + dc * 128:AR_US[gb] + (dc + 1) * 128],
                              C[:, dc * 512:(dc + 1) * 512],
                              dc == 0, dc == NEC - 1, d,
                              inc="T" if dc == NEC - 1 else None)
                gms.append(t_gm)
                ums.append(t_um)
                t_g_free[gb] = t_gm
                t_u_free[gb] = t_um
                t_sil = sc.st(
                    "s",
                    lambda s, hc=hc, pgb=pgb, gsil_o=gsil_o: s.activation(
                        AR[:, gsil_o:gsil_o + 512], pgb, AF.Silu,
                        scale=sgT[:, hc:hc + 1]),
                    deps=[t_gm, t_hc_last if hc <= 1 else None,
                          t_H[hc - 2] if hc >= 2 else None], inc="S")
                tok_pb[gn] = t_sil
                t_hcv = sc.st(
                    "v",
                    lambda v, hc=hc, pub=pub, gsil_o=gsil_o:
                    v.scalar_tensor_tensor(
                        AR[:, AR_H + hc * 512:AR_H + (hc + 1) * 512],
                        pub, suT[:, hc:hc + 1],
                        AR[:, gsil_o:gsil_o + 512], ALU.mult, ALU.mult),
                    deps=[t_um, t_sil], inc="V")
                tok_pb[un] = t_hcv
                t_hc_last = t_hcv
                t_H.append(t_hcv)

            MM(SM[0:1, :], onesb[:, :],
               AR[:, AR_H + 42 * 512:AR_H + 43 * 512], False, False,
               [t_H[42]])
            t_cs = MM(SM[0:1, :], onesb[:, :],
                      AR[:, AR_H + 43 * 512:AR_H + 44 * 512], False, True,
                      [t_H[43]], inc="T")
            t_rowb = sc.st(
                "s", lambda s: s.copy(rowb[0:1, :], SM[0:1, :]),
                deps=[t_cs, tok_rowb_free], inc="S")
            tok_pb["SM"] = t_rowb

            t_d_free = [[], []]
            pc_dep = [[gms[42]], [gms[43]], [gms[43]], [ums[42]]]
            pc_span = [(0, 2048), (2048, 2816), (2816, 4096), (4096, 5632)]
            pc_sem = ("Dd0", "Dd1", "Dd2", "Dd3")
            t_z = []
            xw_n = []
            for ec in range(NEC):
                dn = "O0" if ec % 2 == 0 else "O1"
                pbd = O0 if ec % 2 == 0 else O1
                hw = [None] * 4
                for p in range(4):
                    c0, c1 = pc_span[p]
                    hw[p] = sc.dma(
                        "g",
                        lambda g, ec=ec, c0=c0, c1=c1: g.dma_start(
                            AR[:, c0:c1], wd[ec, :, c0:c1]),
                        deps=pc_dep[p] + [t_d_free[p // 2]],
                        inc=pc_sem[p])
                if ti + 1 < NT and ec < 4:
                    q0n = (ti + 1) * TT
                    xw_n.append(sc.dma(
                        "g",
                        lambda g, q0n=q0n, k=ec: g.dma_start(
                            C[:, k * 2048:(k + 1) * 2048],
                            bass.AP(x_in, q0n + 4 * k * 128 * 2048,
                                    [[2048, 128], [128 * 2048, 4],
                                     [1, 512]])),
                        deps=[t_u_free[1]]
                             + ([xw_n[ec - 2]] if ec >= 2 else []),
                        inc="Dx%d" % (ec % 2)))
                t_dm = None
                for hc in range(NHC):
                    half = 0 if hc < 22 else 1
                    off = AR_DS[half] + (hc - 22 * half) * 128
                    d = [t_H[hc]]
                    if hc == 0:
                        d += [hw[0], tok_pb[dn]]
                    elif hc == 16:
                        d += [hw[1]]
                    elif hc == 22:
                        d += [hw[2]]
                    elif hc == 32:
                        d += [hw[3]]
                    t_dm = MM(pbd[:, :], AR[:, off:off + 128],
                              AR[:, AR_H + hc * 512:AR_H + (hc + 1) * 512],
                              hc == 0, False, d,
                              inc="T" if hc == 21 else None)
                    if hc == 21:
                        t_d_free[0] = t_dm
                t_dm = MM(pbd[:, :], m128b[0:1, :], rowb[0:1, :],
                          False, True, [t_rowb], inc="T")
                tok_rowb_free = t_dm
                t_d_free[1] = t_dm
                d_z = [t_dm]
                t_zc = sc.st(
                    "v",
                    lambda v, ec=ec, pbd=pbd: v.scalar_tensor_tensor(
                        B[:, ec * 512:(ec + 1) * 512], pbd[:, :],
                        sdT[:, ec:ec + 1], A[:, ec * 512:(ec + 1) * 512],
                        ALU.mult, ALU.add),
                    deps=d_z, inc="V")
                tok_pb[dn] = t_zc
                t_z.append(t_zc)

            if ti + 1 < NT:
                hzf = [t_d_free[1]]
                scrf = hzf
                tok_kh_free = [t_d_free[1], t_d_free[1]]
                tok_vtm_free = [t_d_free[1], t_d_free[1]]
                emit_astart(ti + 1, xw_n, hzf, scrf)

            t_out, _, t_sq2, rows_rd2, t_st2_ln2, t_sub2 = layernorm(
                B, A, t_z, AR_CAT, [t_z[-1]], cast_out=False,
                sa_free=[tok_pb["SA0"], tok_pb["SA1"]],
                sb_free=[tok_pb["SB0"], tok_pb["SB1"]],
                out_in_src=True, f32_stats=True)
            tok_rows_free = rows_rd2
            tok_sc_rd = [t_sub2, t_out[-1]]

            t_st = sc.dma(
                "s",
                lambda s, q0=q0: s.dma_start(
                    bass.AP(y_out, q0,
                            [[2048, 128], [128 * 2048, 16], [1, 512]]),
                    B[:, :]),
                deps=t_out, inc="Dy")

            tok_A_free = list(t_out)
            tok_B_free = [t_st]
            tok_sbinv_free = t_d_free[1]
            tok_cat_free = [t_st2_ln2] * NH

        sc.st("q", lambda q: q.nop(),
              deps=[("Dy", sc.counts.get("Dy", 0))])

        with nc.Block() as block:
            @block.tensor
            def _(t):
                sc.replay("t", t, sems)

            @block.vector
            def _(v):
                sc.replay("v", v, sems)

            @block.scalar
            def _(s):
                sc.replay("s", s, sems)

            @block.gpsimd
            def _(g):
                sc.replay("g", g, sems)

            @block.sync
            def _(q):
                sc.replay("q", q, sems)

    return nc


def _host_layouts(inputs):
    perm = np.array(list(range(0, 64, 2)) + list(range(1, 64, 2)))

    def blk_e(w):
        F = w.shape[1]
        return np.ascontiguousarray(
            w.reshape(16, 128, F).transpose(1, 0, 2).reshape(128, 16 * F))

    def blk_h(w):
        return np.ascontiguousarray(
            w.transpose(1, 0, 2).reshape(128, w.shape[0] * w.shape[2]))

    perm_swap = np.concatenate([perm[32:], perm[:32]])
    zc = np.zeros((E, 64), np.float32)
    zh = np.zeros((NH, 128, 64), np.float32)

    d = {}
    d["wdkv"] = blk_e(np.asarray(inputs["L0_attn_W_dkv"], np.float32))
    d["wdq"] = blk_e(np.asarray(inputs["L0_attn_W_dq"], np.float32))
    wkr = np.asarray(inputs["L0_attn_W_kr"], np.float32)
    d["wkrp"] = blk_e(np.concatenate([zc, wkr[:, perm]], 1))
    d["wkrs"] = blk_e(np.concatenate([zc, wkr[:, perm_swap]], 1))
    wqc = np.asarray(inputs["L0_attn_W_qc"], np.float32)
    wqr = np.asarray(inputs["L0_attn_W_qr"], np.float32)
    d["wqcr"] = blk_h(np.concatenate([wqc, wqr[:, :, perm]], 2))
    d["wqsw"] = blk_h(np.concatenate([zh, wqr[:, :, perm_swap]], 2))
    d["wuk"] = blk_h(np.asarray(inputs["L0_attn_W_uk"], np.float32))
    d["wuv"] = blk_h(np.asarray(inputs["L0_attn_W_uv"], np.float32))
    wo = np.asarray(inputs["L0_attn_W_o"], np.float32)
    d["wo"] = np.ascontiguousarray(
        wo.reshape(16, 128, 16, 128).transpose(2, 1, 0, 3).reshape(16, 128, 2048))
    wgq = np.asarray(inputs["L0_ffn_wq_gate"], np.int32).astype(np.uint8).T
    d["wg"] = np.ascontiguousarray(
        wgq.reshape(16, 128, 44, 128).transpose(2, 1, 0, 3).reshape(44, 128, 2048))
    wuq = np.asarray(inputs["L0_ffn_wq_up"], np.int32).astype(np.uint8).T
    d["wu"] = np.ascontiguousarray(
        wuq.reshape(16, 128, 44, 128).transpose(2, 1, 0, 3).reshape(44, 128, 2048))
    wdn = np.asarray(inputs["L0_ffn_wq_down"], np.int32).astype(np.uint8).T
    d["wd"] = np.ascontiguousarray(
        wdn.reshape(44, 128, 16, 128).transpose(2, 1, 0, 3).reshape(16, 128, 5632))
    d["sg"] = np.ascontiguousarray(
        np.asarray(inputs["L0_ffn_s_gate"], np.float32).reshape(44, 128).T)
    d["su"] = np.ascontiguousarray(
        np.asarray(inputs["L0_ffn_s_up"], np.float32).reshape(44, 128).T)
    d["sd"] = np.ascontiguousarray(
        np.asarray(inputs["L0_ffn_s_down"], np.float32).reshape(16, 128).T)

    half = DR // 2
    freqs = np.exp(np.arange(half, dtype=np.float64) *
                   (-np.log(10000.0) / half))
    ang = np.arange(S, dtype=np.float64)[:, None] * freqs[None, :]
    cos = np.cos(ang).T.astype(np.float32)
    sin = np.sin(ang).T.astype(np.float32)
    rt = np.zeros((128, 2 * S), np.float32)
    rt[64:96, 0:S] = cos
    rt[96:128, 0:S] = cos
    rt[64:96, S:2 * S] = -sin
    rt[96:128, S:2 * S] = sin
    d["rtab"] = rt

    p = np.arange(128)[:, None]
    j = np.arange(896)[None, :]
    d["mks"] = np.where(p <= j - 384, 0.0, -1e30).astype(np.float32)
    d["idm"] = np.eye(128, dtype=np.float32)
    return d


def kernel(L0_attn_x, **weights):
    weights["L0_attn_x"] = L0_attn_x
    shared = _host_layouts(weights)
    x = np.asarray(L0_attn_x, np.float32)
    in_maps = []
    for b in range(8):
        m = dict(shared)
        m["x"] = np.ascontiguousarray(x[b].T)
        in_maps.append(m)
    nc = bass.Bass("TRN2", target_bir_lowering=False)
    build_program(nc)
    res = run_bass_kernel_spmd(nc, in_maps, core_ids=list(range(8)))
    out = np.stack([np.asarray(r["y"]).T for r in res.results], 0)
    return out.astype(np.float32)


if __name__ == "__main__":
    nc_ = bass.Bass("TRN2", target_bir_lowering=False)
    build_program(nc_)
    try:
        n = len(list(nc_.all_instructions()))
    except Exception:
        n = -1
    print("program built, instructions:", n)


# version 42 (final)
# speedup vs baseline: 1.0556x; 1.0059x over previous
import numpy as np
import concourse.bass as bass
import concourse.mybir as mybir
from concourse.bass_utils import run_bass_kernel_spmd

AF = mybir.ActivationFunctionType
ALU = mybir.AluOpType
F32 = mybir.dt.float32
BF16 = mybir.dt.bfloat16
I32 = mybir.dt.int32
U8 = mybir.dt.uint8

S = 2048
E = 2048
NH = 16
DR = 64
DH = 128
HID = 5632
NHC = HID // 128
NEC = E // 128
TT = 512
NT = S // TT
QSCALE = float(1.0 / np.sqrt(DH))

ENGINES = ("t", "v", "s", "g", "q")


class Sched:

    def __init__(self):
        self.steps = {e: [] for e in ENGINES}
        self.counts = {}

    def st(self, eng, fn, deps=(), inc=None, amt=1):
        waits = []
        for d in deps:
            if d is None:
                continue
            if isinstance(d, list):
                waits += [x for x in d if x is not None]
            else:
                waits.append(d)
        tok = None
        if inc is not None:
            self.counts[inc] = self.counts.get(inc, 0) + amt
            tok = (inc, self.counts[inc])
        self.steps[eng].append((fn, waits, inc, amt))
        return tok

    def dma(self, eng, fn, deps=(), inc=None):
        return self.st(eng, fn, deps, inc, 16)

    def replay(self, eng_name, eng, sems):
        wm = {}
        for fn, waits, inc, amt in self.steps[eng_name]:
            for (sname, val) in waits:
                if wm.get(sname, 0) < val:
                    eng.wait_ge(sems[sname], val)
                    wm[sname] = val
            inst = fn(eng)
            if inc is not None:
                inst.then_inc(sems[inc], amt)


AR_N = 31744
AR_KH = (0, 2048)
AR_VT = (4096, 6144)
AR_ET = (8192, 9216)
AR_XDQ = 8192
AR_QA = 10240
AR_CAT = 18432
AR_SBI = 26624
AR_WO = (0, 2048)
AR_Y2 = 4096
AR_GS = (0, 2048)
AR_US = (4096, 6144)
AR_H = 8192
AR_DS = (0, 2816)
AR_GSIL = 30720


def build_program(nc):
    x_in = nc.dram_tensor("x", [E, S], F32, kind="ExternalInput")
    wdkv = nc.dram_tensor("wdkv", [128, NEC * 128], F32, kind="ExternalInput")
    wdq = nc.dram_tensor("wdq", [128, NEC * 128], F32, kind="ExternalInput")
    wkrp = nc.dram_tensor("wkrp", [128, NEC * 128], F32, kind="ExternalInput")
    wkrs = nc.dram_tensor("wkrs", [128, NEC * 128], F32, kind="ExternalInput")
    wqcr = nc.dram_tensor("wqcr", [128, NH * 128], F32, kind="ExternalInput")
    wqsw = nc.dram_tensor("wqsw", [128, NH * 128], F32, kind="ExternalInput")
    wuk = nc.dram_tensor("wuk", [128, NH * 64], F32, kind="ExternalInput")
    wuv = nc.dram_tensor("wuv", [128, NH * 128], F32, kind="ExternalInput")
    rtab = nc.dram_tensor("rtab", [128, 2 * S], F32, kind="ExternalInput")
    wo = nc.dram_tensor("wo", [NEC, 128, NEC * 128], F32, kind="ExternalInput")
    wg = nc.dram_tensor("wg", [NHC, 128, NEC * 128], U8, kind="ExternalInput")
    wu = nc.dram_tensor("wu", [NHC, 128, NEC * 128], U8, kind="ExternalInput")
    wd = nc.dram_tensor("wd", [NEC, 128, NHC * 128], U8, kind="ExternalInput")
    sg_in = nc.dram_tensor("sg", [128, NHC], F32, kind="ExternalInput")
    su_in = nc.dram_tensor("su", [128, NHC], F32, kind="ExternalInput")
    sd_in = nc.dram_tensor("sd", [128, NEC], F32, kind="ExternalInput")
    mks = nc.dram_tensor("mks", [128, 896], F32, kind="ExternalInput")
    idm = nc.dram_tensor("idm", [128, 128], F32, kind="ExternalInput")
    y_out = nc.dram_tensor("y", [E, S], F32, kind="ExternalOutput")

    wo_b = nc.dram_tensor("wo_b", [NEC, 128, NEC * 128], BF16, kind="Internal")

    sc = Sched()

    from contextlib import ExitStack
    with ExitStack() as ctx:
        def sb(name, shape, dt):
            return ctx.enter_context(nc.sbuf_tensor(name, shape, dt))

        def ps(name):
            return ctx.enter_context(nc.psum_tensor(name, [128, 512], F32))

        A = sb("A", [128, 8192], F32)
        B = sb("B", [128, 8192], F32)
        C = sb("C", [128, 8192], BF16)
        AR = sb("AR", [128, AR_N], BF16)
        ckv = sb("ckv", [128, S], BF16)
        krc = sb("krc", [128, S], BF16)
        rtabb = sb("rtabb", [128, 2 * S], BF16)
        mkb = sb("mkb", [128, 896], BF16)
        idb = sb("idb", [128, 128], BF16)
        w_dkv = sb("w_dkv", [128, NEC * 128], BF16)
        w_dq = sb("w_dq", [128, NEC * 128], BF16)
        w_krp = sb("w_krp", [128, NEC * 128], BF16)
        w_krs = sb("w_krs", [128, NEC * 128], BF16)
        w_qcr = sb("w_qcr", [128, NH * 128], BF16)
        w_qsw = sb("w_qsw", [128, NH * 128], BF16)
        w_uk = sb("w_uk", [128, NH * 64], BF16)
        w_uv = sb("w_uv", [128, NH * 128], BF16)
        sgT = sb("sgT", [128, NHC], F32)
        suT = sb("suT", [128, NHC], F32)
        sdT = sb("sdT", [128, NEC], F32)
        rows = sb("rows", [1, 1536], F32)
        onesb = sb("onesb", [128, 1], BF16)
        onesf = sb("onesf", [1, 128], F32)
        onesf2 = sb("onesf2", [128, 1], F32)
        m128b = sb("m128b", [1, 128], BF16)
        rowb = sb("rowb", [1, 512], BF16)
        SA = ctx.enter_context(nc.psum_tensor("SA", [128, 1024], F32))
        SB = ctx.enter_context(nc.psum_tensor("SB", [128, 1024], F32))
        O0, O1, SM, KV = ps("O0"), ps("O1"), ps("SM"), ps("KV")
        sem_names = ["T", "V", "S", "G", "Dx0", "Dx1", "Dx2", "Dx3", "Dsw0", "Dsw1", "Dsc",
                     "Dst0", "Dst1",
                     "Dwo0", "Dwo1", "Dg0", "Dg1", "Du0", "Du1",
                     "Dd0", "Dd1", "Dd2", "Dd3", "Dy"]
        sems = {n: ctx.enter_context(nc.semaphore("sem" + n))
                for n in sem_names}
        SA0, SA1 = SA[:, 0:512], SA[:, 512:1024]
        SB0, SB1 = SB[:, 0:512], SB[:, 512:1024]
        sbinv = AR[:, AR_SBI:AR_SBI + 1024].bitcast(F32)

        def MM(psum_ap, lhsT_ap, rhs_ap, start, stop, deps, inc=None):
            return sc.st(
                "t",
                lambda t, o=psum_ap, l=lhsT_ap, r=rhs_ap, a=start, b=stop:
                    t.matmul(o, l, r, start=a, stop=b),
                deps=deps, inc=inc)

        t_ones_b = sc.st("v", lambda v: v.memset(onesb[:], 1.0), inc="V")
        t_ones_f = sc.st("v", lambda v: v.memset(onesf[:], 1.0), inc="V")
        t_ones_f2 = sc.st("v", lambda v: v.memset(onesf2[:], 1.0), inc="V")
        t_m128 = sc.st("v", lambda v: v.memset(m128b[:], -128.0), inc="V")
        sc.dma("q", lambda q: q.dma_start(sgT[:], sg_in.ap()), inc="Dsc")
        sc.dma("q", lambda q: q.dma_start(suT[:], su_in.ap()), inc="Dsc")
        t_sd = ("Dsc", 48)
        sc.dma("q", lambda q: q.dma_start(sdT[:], sd_in.ap()), inc="Dsc")

        small_w = [
            (rtab, rtabb[:, :], 4096), (mks, mkb[:, :], 896),
            (idm, idb[:, :], 128),
            (wdkv, w_dkv[:, :], 2048), (wdq, w_dq[:, :], 2048),
            (wkrp, w_krp[:, :], 2048), (wkrs, w_krs[:, :], 2048),
            (wqcr, w_qcr[:, :], 2048), (wqsw, w_qsw[:, :], 2048),
            (wuk, w_uk[:, :], 1024), (wuv, w_uv[:, :], 2048),
        ]
        half_free = [None, None]
        for i, (src, dst, n) in enumerate(small_w):
            hb = i % 2
            off = hb * 4096
            t_ld = sc.dma(
                "q",
                lambda q, src=src, off=off, n=n: q.dma_start(
                    B[:, off:off + n], src.ap()),
                deps=[half_free[hb]], inc="Dsw%d" % hb)
            t_cv = sc.st(
                "v",
                lambda v, dst=dst, off=off, n=n: v.tensor_copy(
                    dst, B[:, off:off + n]),
                deps=[t_ld], inc="V")
            half_free[hb] = t_cv

        store_tok = {}
        for ec in range(NEC):
            hb = ec % 2
            off = hb * 4096
            t_ld = sc.dma(
                "q",
                lambda q, ec=ec, off=off: q.dma_start(
                    B[:, off:off + 2048], wo[ec, :, :]),
                deps=[half_free[hb]], inc="Dsw%d" % hb)
            cto = AR_CAT + hb * 2048
            t_cv = sc.st(
                "s",
                lambda s, off=off, cto=cto: s.activation(
                    AR[:, cto:cto + 2048], B[:, off:off + 2048],
                    AF.Copy),
                deps=[t_ld], inc="S")
            t_stb = sc.dma(
                "s",
                lambda s, ec=ec, cto=cto: s.dma_start(
                    wo_b[ec, :, :], AR[:, cto:cto + 2048]),
                deps=[t_cv], inc="Dst%d" % hb)
            half_free[hb] = t_stb
            store_tok.setdefault(("wo_b", ec), []).append(t_stb)


        prep_done = [t for t in half_free if t is not None] + [
            t_sd, t_ones_b, t_ones_f, t_ones_f2, t_m128,
            ("Dst0", sc.counts.get("Dst0", 0)),
            ("Dst1", sc.counts.get("Dst1", 0))]

        tok_A_free = list(prep_done)
        tok_B_free = list(prep_done)
        tok_C_free = list(prep_done)
        tok_pb = {k: None for k in
                  ("SA0", "SA1", "SB0", "SB1", "O0", "O1", "SM", "KV")}
        tok_cat_free = [[("Dst0", sc.counts.get("Dst0", 0)),
                         ("Dst1", sc.counts.get("Dst1", 0))]] * NH
        tok_kh_free = [None, None]
        tok_vtm_free = [None, None]
        tok_qall_free = None
        tok_rows_free = []
        tok_sbinv_free = None
        tok_sc_rd = [None, None]
        tok_expt_rd = [None, None]
        tok_rowb_free = None
        t_hc_last = None

        kv_tok = [[], []]
        carry = {}
        scrA = AR[64:128, 0:512]
        scrB = AR[64:128, 512:1024]
        kh_waits_init = [None]

        def make_kv_items(hh, ctx2, cache_w, q_ev):
            par = hh % 2
            khs = AR_KH[par]
            vts = AR_VT[par]
            nc52 = ctx2 // 512
            items = []

            def k_piece(c5, hh=hh, par=par, khs=khs):
                t_kn = MM(KV[0:64, :], w_uk[:, hh * 64:(hh + 1) * 64],
                          ckv[:, c5 * 512:(c5 + 1) * 512], True, True,
                          [tok_pb["KV"]] + (cache_w + q_ev
                                            if hh <= 1 and c5 == 0 else []),
                          inc="T")
                t_kn_e = sc.st(
                    "s",
                    lambda s, khs=khs, c5=c5: s.copy(
                        AR[0:64, khs + c5 * 512:khs + (c5 + 1) * 512],
                        KV[0:64, :]),
                    deps=[t_kn, tok_kh_free[par]], inc="S")
                tok_pb["KV"] = t_kn_e
                kv_tok[par].append(t_kn_e)

            def kr_piece(hh=hh, par=par, khs=khs, ctx2=ctx2):
                t_krcp = sc.st(
                    "v",
                    lambda v, khs=khs, ctx2=ctx2: v.tensor_copy(
                        AR[64:128, khs:khs + ctx2], krc[64:128, 0:ctx2]),
                    deps=[tok_kh_free[par]] + cache_w
                         + ([q_ev[-1]] if hh <= 1 else []), inc="V")
                kv_tok[par].append(t_krcp)

            def v_piece(tg, hh=hh, par=par, vts=vts):
                t_v = None
                for i in range(4):
                    tc = tg * 4 + i
                    t_v = MM(KV[:, i * 128:(i + 1) * 128],
                             ckv[:, tc * 128:(tc + 1) * 128],
                             w_uv[:, hh * 128:(hh + 1) * 128],
                             True, True,
                             [tok_pb["KV"]] if i == 0 else [],
                             inc="T" if i == 3 else None)
                t_v_e = sc.st(
                    "v",
                    lambda v, vts=vts, tg=tg: v.tensor_copy(
                        AR[:, vts + tg * 512:vts + (tg + 1) * 512],
                        KV[:, :]),
                    deps=[t_v, tok_vtm_free[par]], inc="V")
                tok_pb["KV"] = t_v_e
                kv_tok[par].append(t_v_e)

            for c5 in range(nc52):
                items.append(lambda c5=c5: k_piece(c5))
            items.append(kr_piece)
            for tg in range(nc52):
                items.append(lambda tg=tg: v_piece(tg))
            return items

        def emit_xloads(ti2, base_deps):
            q02 = ti2 * TT
            toks = []
            for k in range(4):
                toks.append(sc.dma(
                    "g",
                    lambda g, q02=q02, k=k: g.dma_start(
                        C[:, k * 2048:(k + 1) * 2048],
                        bass.AP(x_in, q02 + 4 * k * 128 * 2048,
                                [[2048, 128], [128 * 2048, 4], [1, 512]])),
                    deps=base_deps + ([toks[k - 2]] if k >= 2 else []),
                    inc="Dx%d" % (k % 2)))
            return toks

        def emit_astart(ti2, xw, dep_hz, dep_scr):
            q02 = ti2 * TT
            for kc in range(NEC):
                d = [tok_pb["SA0"]] if kc == 0 else []
                d += [xw[kc // 4]] if kc % 4 == 0 else []
                t_ckv = MM(SA0, w_dkv[:, kc * 128:(kc + 1) * 128],
                           C[:, kc * 512:(kc + 1) * 512],
                           kc == 0, kc == NEC - 1, d,
                           inc="T" if kc == NEC - 1 else None)
            for kc in range(NEC):
                d = [tok_pb["SA1"]] if kc == 0 else []
                t_xdq = MM(SA1, w_dq[:, kc * 128:(kc + 1) * 128],
                           C[:, kc * 512:(kc + 1) * 512],
                           kc == 0, kc == NEC - 1, d,
                           inc="T" if kc == NEC - 1 else None)
            for kc in range(NEC):
                d = [tok_pb["SB0"]] if kc == 0 else []
                t_krp = MM(SB0, w_krp[:, kc * 128:(kc + 1) * 128],
                           C[:, kc * 512:(kc + 1) * 512],
                           kc == 0, kc == NEC - 1, d,
                           inc="T" if kc == NEC - 1 else None)
            for kc in range(NEC):
                d = [tok_pb["SB1"]] if kc == 0 else []
                t_krs = MM(SB1, w_krs[:, kc * 128:(kc + 1) * 128],
                           C[:, kc * 512:(kc + 1) * 512],
                           kc == 0, kc == NEC - 1, d,
                           inc="T" if kc == NEC - 1 else None)
            c_last = t_krs

            t_ckv_e = sc.st(
                "s",
                lambda s, q02=q02: s.copy(ckv[:, q02:q02 + 512], SA0),
                deps=[t_ckv], inc="S")
            tok_pb["SA0"] = t_ckv_e
            t_xdq_e = sc.st(
                "s",
                lambda s: s.copy(AR[:, AR_XDQ:AR_XDQ + 512], SA1),
                deps=[t_xdq, tok_qall_free, tok_sc_rd[0]] + dep_hz, inc="S")
            tok_pb["SA1"] = t_xdq_e

            ccl = rtabb[64:128, q02:q02 + 512]
            ssl = rtabb[64:128, S + q02:S + q02 + 512]
            t_k1 = sc.st(
                "v",
                lambda v, ccl=ccl: v.tensor_tensor(
                    scrA, SB[64:128, 0:512], ccl, ALU.mult),
                deps=[t_krp] + dep_scr, inc="V")
            t_k2 = sc.st(
                "v",
                lambda v, ssl=ssl: v.tensor_tensor(
                    scrB, SB[64:128, 512:1024], ssl, ALU.mult),
                deps=[t_krs], inc="V")
            t_k3 = sc.st(
                "v",
                lambda v, q02=q02: v.tensor_tensor(
                    krc[64:128, q02:q02 + 512], scrA, scrB, ALU.add),
                deps=[t_k1, t_k2], inc="V")
            tok_pb["SB0"] = t_k1
            tok_pb["SB1"] = t_k2
            cache_w = [t_ckv_e, t_k3]

            t_q_ev = []
            pb_rd = {"O0": [tok_pb["O0"]], "O1": [tok_pb["O1"]],
                     "SM": [tok_pb["SM"]], "KV": [tok_pb["KV"]]}
            prev_add = None
            for h in range(NH):
                na = "O0" if h % 2 == 0 else "O1"
                nb = "SM" if h % 2 == 0 else "KV"
                Pa = O0 if h % 2 == 0 else O1
                Pb = SM if h % 2 == 0 else KV
                t_q1 = MM(Pa[:, :], w_qcr[:, h * 128:(h + 1) * 128],
                          AR[:, AR_XDQ:AR_XDQ + 512], True, True,
                          [t_xdq_e] + pb_rd[na], inc="T")
                t_q2 = MM(Pb[:, :], w_qsw[:, h * 128:(h + 1) * 128],
                          AR[:, AR_XDQ:AR_XDQ + 512], True, True,
                          pb_rd[nb], inc="T")
                t_qn_e = sc.st(
                    "s",
                    lambda s, h=h, b=Pa: s.copy(
                        AR[0:64, AR_QA + h * 512:AR_QA + h * 512 + 512],
                        b[0:64, :]),
                    deps=[t_q1, tok_qall_free], inc="S")
                t_m1 = sc.st(
                    "v",
                    lambda v, ccl=ccl, b=Pa: v.tensor_tensor(
                        scrA, b[64:128, :], ccl, ALU.mult),
                    deps=[t_q1, prev_add, t_k3 if h == 0 else None],
                    inc="V")
                t_m2 = sc.st(
                    "v",
                    lambda v, ssl=ssl, b=Pb: v.tensor_tensor(
                        scrB, b[64:128, :], ssl, ALU.mult),
                    deps=[t_q2], inc="V")
                t_add = sc.st(
                    "v",
                    lambda v, h=h: v.tensor_tensor(
                        AR[64:128, AR_QA + h * 512:AR_QA + h * 512 + 512],
                        scrA, scrB, ALU.add),
                    deps=[t_m1, t_m2, tok_qall_free], inc="V")
                prev_add = t_add
                pb_rd[na] = [t_qn_e, t_m1]
                pb_rd[nb] = [t_m2]
                t_q_ev += [t_qn_e, t_add]
            for k2 in ("O0", "O1", "SM", "KV"):
                tok_pb[k2] = pb_rd[k2]

            ctx2 = (ti2 + 1) * TT
            for hh in (0, 1):
                for it in make_kv_items(hh, ctx2, cache_w, t_q_ev):
                    it()
            carry["ast"] = {"q_ev": t_q_ev, "cache_w": cache_w,
                            "ccl": ccl, "ssl": ssl, "c_last": c_last}

        for ti in range(NT):
            q0 = ti * TT
            ctx = (ti + 1) * TT
            nc5 = ctx // 512
            ntc = ctx // 128
            npair = ntc // 2

            if ti == 0:
                xw0 = emit_xloads(0, list(prep_done))
                emit_astart(0, xw0, [tok_qall_free], kh_waits_init)
            ast = carry.pop("ast")
            t_q_ev = ast["q_ev"]
            tok_cache_w = ast["cache_w"]
            ccl = ast["ccl"]
            ssl = ast["ssl"]
            tok_C_free = [ast["c_last"]]

            t_xr = []
            for j in range(2):
                t_xr.append(sc.dma(
                    "q",
                    lambda q, q0=q0, j=j: q.dma_start(
                        A[:, j * 4096:(j + 1) * 4096],
                        bass.AP(x_in, q0 + 8 * j * 128 * 2048,
                                [[2048, 128], [128 * 2048, 8], [1, 512]])),
                    deps=tok_A_free, inc="Dx%d" % (2 + j)))

            pending_tail = None
            pending_tail_b = None
            st8 = {"rows": tok_rows_free, "sbinv": tok_sbinv_free,
                   "si": None}
            for h in range(NH):
                kb = h % 2
                oname = "O0" if kb == 0 else "O1"
                O = O0 if kb == 0 else O1
                khs = AR_KH[kb]
                vts = AR_VT[kb]
                pending = (make_kv_items(h + 1, ctx, tok_cache_w, t_q_ev)
                           if 1 <= h < NH - 1 else [])
                first_deps = kv_tok[kb]
                kv_tok[kb] = []

                t_av = None
                t_sum = None
                t_exp = [None, None]
                for tp in range(npair + 1):
                    if tp < npair:
                        sb = tp % 2
                        SC = SA if sb == 0 else SB
                        etb = AR_ET[sb]
                        for i in range(2):
                            tc = 2 * tp + i
                            d = [tok_sc_rd[sb]] + first_deps
                            first_deps = []
                            diag = tc >= 4 * ti
                            t_scm = MM(SC[:, i * 512:(i + 1) * 512],
                                       AR[:, khs + tc * 128:khs + (tc + 1) * 128],
                                       AR[:, AR_QA + h * 512:AR_QA + (h + 1) * 512],
                                       True, not diag, d,
                                       inc=None if diag else "T")
                            if diag:
                                kk = 384 - 128 * (tc - 4 * ti)
                                t_scm = MM(SC[:, i * 512:(i + 1) * 512],
                                           idb[:, :],
                                           mkb[:, kk:kk + 512],
                                           False, True, [], inc="T")
                        t_exp[sb] = sc.st(
                            "s",
                            lambda s, SC=SC, etb=etb: s.activation(
                                AR[:, etb:etb + 1024], SC[:, 0:1024], AF.Exp,
                                scale=np.float32(QSCALE)),
                            deps=[t_scm, tok_expt_rd[sb]], inc="S")
                        tok_sc_rd[sb] = t_exp[sb]
                        if tp == 0 and pending_tail is not None:
                            pending_tail()
                            pending_tail = None
                    pa = tp - 1
                    if pa >= 0:
                        sbp = pa % 2
                        etbp = AR_ET[sbp]
                        for i in range(2):
                            tc = 2 * pa + i
                            d_av = [t_exp[sbp]]
                            if tc == 0:
                                d_av += [tok_pb[oname]]
                            t_av = MM(O[:, :],
                                      AR[:, vts + tc * 128:vts + (tc + 1) * 128],
                                      AR[:, etbp + i * 512:etbp + (i + 1) * 512],
                                      tc == 0, tc == ntc - 1, d_av, inc="T")
                        if pa == 0 and pending_tail_b is not None:
                            pending_tail_b()
                            pending_tail_b = None
                        for i in range(2):
                            tc = 2 * pa + i
                            d_sm = [tok_pb["SM"]] if tc == 0 else []
                            t_sum = MM(SM[0:1, :], onesb[:, :],
                                       AR[:, etbp + i * 512:etbp + (i + 1) * 512],
                                       tc == 0, tc == ntc - 1, d_sm, inc="T")
                        tok_expt_rd[sbp] = t_sum
                    if pending:
                        nleft = npair + 1 - tp
                        npop = (len(pending) + nleft - 1) // nleft
                        for _ in range(npop):
                            if pending:
                                pending.pop(0)()
                while pending:
                    pending.pop(0)()
                tok_kh_free[kb] = t_scm
                tok_vtm_free[kb] = t_av

                def mk_tail_a(t_sum=t_sum):
                    def tail_a():
                        t_ln = sc.st(
                            "s",
                            lambda s: s.activation(rows[0:1, 512:1024],
                                                   SM[0:1, :], AF.Ln),
                            deps=[t_sum] + st8["rows"], inc="S")
                        t_si = sc.st(
                            "s",
                            lambda s: s.activation(rows[0:1, 1024:1536],
                                                   rows[0:1, 512:1024],
                                                   AF.Exp,
                                                   scale=np.float32(-1.0)),
                            deps=[t_ln], inc="S")
                        st8["si"] = t_si
                    return tail_a

                def mk_tail_b(h=h, O=O, oname=oname, t_av=t_av):
                    def tail_b():
                        t_bc = MM(SM[:, :], onesf[0:1, :],
                                  rows[0:1, 1024:1536],
                                  True, True, [st8["si"]], inc="T")
                        st8["rows"] = [t_bc]
                        t_sbe = sc.st(
                            "v", lambda v: v.tensor_copy(sbinv, SM[:, :]),
                            deps=[t_bc, st8["sbinv"]], inc="V")
                        tok_pb["SM"] = t_sbe
                        t_mu = sc.st(
                            "v",
                            lambda v: v.tensor_tensor(
                                AR[:, AR_CAT + h * 512:
                                   AR_CAT + (h + 1) * 512],
                                O[:, :], sbinv, ALU.mult),
                            deps=[t_av, t_sbe, tok_cat_free[h]], inc="V")
                        tok_pb[oname] = t_mu
                        st8["sbinv"] = t_mu
                        tok_cat_free[h] = t_mu
                    return tail_b

                pending_tail = mk_tail_a()
                pending_tail_b = mk_tail_b()

            pending_tail()
            pending_tail_b()
            pending_tail = None
            pending_tail_b = None
            tok_rows_free = st8["rows"]
            tok_sbinv_free = st8["sbinv"]
            tok_qall_free = t_scm

            t_wo_free = [None, None]
            t_w2 = []
            for ec in range(NEC):
                wb = ec % 2
                dd = [t_wo_free[wb], tok_kh_free[wb]]
                if ti == 0:
                    dd += store_tok[("wo_b", ec)]
                t_ld = sc.dma(
                    "q",
                    lambda q, ec=ec, wb=wb: q.dma_start(
                        AR[:, AR_WO[wb]:AR_WO[wb] + 2048], wo_b[ec, :, :]),
                    deps=dd, inc="Dwo%d" % wb)
                wn = "SA0" if ec % 2 == 0 else "SB0"
                pbw = SA0 if ec % 2 == 0 else SB0
                for k in range(NEC):
                    d = [t_ld, tok_cat_free[k]]
                    if k == 0:
                        d += [tok_pb[wn], tok_sc_rd[ec % 2]]
                    t_mm = MM(pbw,
                              AR[:, AR_WO[wb] + k * 128:AR_WO[wb] + (k + 1) * 128],
                              AR[:, AR_CAT + k * 512:AR_CAT + (k + 1) * 512],
                              k == 0, k == NEC - 1, d,
                              inc="T" if k == NEC - 1 else None)
                t_wo_free[wb] = t_mm
                t_add = sc.st(
                    "v",
                    lambda v, ec=ec, pbw=pbw: v.tensor_tensor(
                        B[:, ec * 512:(ec + 1) * 512], pbw,
                        A[:, ec * 512:(ec + 1) * 512], ALU.add),
                    deps=[t_mm, t_xr[ec // 8]]
                         + (tok_B_free if ec == 0 else []), inc="V")
                tok_pb[wn] = t_add
                t_w2.append(t_add)

            def layernorm(src_f32, dst_f32, t_src_ready, y2_off, y2_free,
                          cast_out, sa_free, sb_free,
                          out_in_src=False, f32_stats=False):
                t_cast = []
                t_sq = []
                for j in range(4):
                    dj = (t_src_ready[4 * j:4 * j + 4]
                          if len(t_src_ready) == 16 else t_src_ready)
                    if not f32_stats:
                        t_cast.append(sc.st(
                            "s",
                            lambda s, j=j: s.activation(
                                C[:, j * 2048:(j + 1) * 2048],
                                src_f32[:, j * 2048:(j + 1) * 2048], AF.Copy),
                            deps=dj + (tok_C_free if j == 0 else []),
                            inc="S"))
                    t_sq.append(sc.st(
                        "s",
                        lambda s, j=j: s.activation(
                            AR[:, y2_off + j * 2048:y2_off + (j + 1) * 2048],
                            src_f32[:, j * 2048:(j + 1) * 2048], AF.Square),
                        deps=dj + (y2_free if j == 0 else []), inc="S"))
                for kc in range(NEC):
                    if f32_stats:
                        d1 = ([t_src_ready[kc]] if len(t_src_ready) == 16
                              else list(t_src_ready) if kc == 0 else [])
                        if kc == 0:
                            d1 += [tok_pb["SM"]]
                        t_st1 = MM(SM[0:1, :], onesf2[:, :],
                                   src_f32[:, kc * 512:(kc + 1) * 512],
                                   kc == 0, kc == NEC - 1, d1,
                                   inc="T" if kc == NEC - 1 else None)
                        continue
                    d1 = [t_cast[kc // 4]] if kc % 4 == 0 else []
                    if kc == 0:
                        d1 += [tok_pb["SM"]]
                    t_st1 = MM(SM[0:1, :], onesb[:, :],
                               C[:, kc * 512:(kc + 1) * 512],
                               kc == 0, kc == NEC - 1, d1,
                               inc="T" if kc == NEC - 1 else None)
                for kc in range(NEC):
                    d2 = [t_sq[kc // 4]] if kc % 4 == 0 else []
                    if kc == 0:
                        d2 += [tok_pb["KV"]]
                    t_st2 = MM(KV[0:1, :], onesb[:, :],
                               AR[:, y2_off + kc * 512:y2_off + (kc + 1) * 512],
                               kc == 0, kc == NEC - 1, d2,
                               inc="T" if kc == NEC - 1 else None)
                t_mean = sc.st(
                    "v",
                    lambda v: v.tensor_scalar(rows[0:1, 0:512], SM[0:1, :],
                                              1.0 / E, None, ALU.mult),
                    deps=[t_st1] + tok_rows_free, inc="V")
                tok_pb["SM"] = t_mean
                t_ey2 = sc.st(
                    "v",
                    lambda v: v.tensor_scalar(rows[0:1, 512:1024],
                                              KV[0:1, :],
                                              1.0 / E, None, ALU.mult),
                    deps=[t_st2], inc="V")
                tok_pb["KV"] = t_ey2
                t_bcm = [MM(SA0, onesf[0:1, :], rows[0:1, 0:512],
                            True, True, [t_mean] + sa_free, inc="T"),
                         MM(SA1, onesf[0:1, :], rows[0:1, 0:512],
                            True, True, [], inc="T")]
                t_m2 = sc.st(
                    "s",
                    lambda s: s.activation(rows[0:1, 1024:1536],
                                           rows[0:1, 0:512], AF.Square),
                    deps=[t_mean], inc="S")
                t_var = sc.st(
                    "v",
                    lambda v: v.tensor_tensor(rows[0:1, 0:512],
                                              rows[0:1, 512:1024],
                                              rows[0:1, 1024:1536],
                                              ALU.subtract),
                    deps=[t_ey2, t_m2] + t_bcm, inc="V")
                t_lnv = sc.st(
                    "s",
                    lambda s: s.activation(rows[0:1, 512:1024],
                                           rows[0:1, 0:512], AF.Ln,
                                           bias=np.float32(1e-5)),
                    deps=[t_var], inc="S")
                t_rstd = sc.st(
                    "s",
                    lambda s: s.activation(rows[0:1, 1024:1536],
                                           rows[0:1, 512:1024], AF.Exp,
                                           scale=np.float32(-0.5)),
                    deps=[t_lnv], inc="S")
                t_bcr = [MM(SB0, onesf[0:1, :], rows[0:1, 1024:1536],
                            True, True, [t_rstd] + sb_free, inc="T"),
                         MM(SB1, onesf[0:1, :], rows[0:1, 1024:1536],
                            True, True, [], inc="T")]
                t_norm = []
                t_co = [] if cast_out else None
                t_s1 = None
                out_t = src_f32 if out_in_src else dst_f32
                for m in range(8):
                    dsr = ([t_src_ready[2 * m], t_src_ready[2 * m + 1]]
                           if len(t_src_ready) == 16 else
                           (list(t_src_ready) if m == 0 else []))
                    t_s1 = sc.st(
                        "v",
                        lambda v, m=m: v.tensor_tensor(
                            dst_f32[:, m * 1024:(m + 1) * 1024],
                            src_f32[:, m * 1024:(m + 1) * 1024],
                            SA[:, 0:1024], ALU.subtract),
                        deps=dsr + (t_bcm + t_bcr if m == 0 else []),
                        inc="V")
                    t_s2 = sc.st(
                        "v",
                        lambda v, m=m, out_t=out_t: v.tensor_tensor(
                            out_t[:, m * 1024:(m + 1) * 1024],
                            dst_f32[:, m * 1024:(m + 1) * 1024],
                            SB[:, 0:1024], ALU.mult),
                        deps=[t_s1], inc="V")
                    t_norm.append(t_s2)
                    if cast_out:
                        t_co.append(sc.st(
                            "s",
                            lambda s, m=m: s.activation(
                                C[:, m * 1024:(m + 1) * 1024],
                                dst_f32[:, m * 1024:(m + 1) * 1024],
                                AF.Copy),
                            deps=[t_s2] + (tok_C_free if m == 0 else []),
                            inc="S"))
                return t_norm, t_co, t_sq[-1], t_bcm + t_bcr, t_st2, t_s1

            t_norm1, t_y16s, t_sq1, rows_rd1, _, t_sub1 = layernorm(
                B, A, t_w2, AR_H, [], cast_out=True,
                sa_free=[tok_pb["SA0"], tok_pb["SA1"], tok_sc_rd[0]],
                sb_free=[tok_pb["SB0"], tok_pb["SB1"], tok_sc_rd[1]])
            tok_rows_free = rows_rd1
            tok_C_free = [t_y16s[-1]]
            tok_pb["SA0"] = tok_pb["SA1"] = t_sub1
            tok_pb["SB0"] = tok_pb["SB1"] = t_norm1[-1]


            t_g_free = [t_wo_free[0], t_wo_free[1]]
            t_u_free = [tok_vtm_free[0], tok_vtm_free[1]]
            t_H = []
            gms = []
            ums = []

            for hc in range(NHC):
                gb = hc % 2
                t_gl = sc.dma(
                    "g",
                    lambda g, hc=hc, gb=gb: g.dma_start(
                        AR[:, AR_GS[gb]:AR_GS[gb] + 2048],
                        wg[hc, :, :]),
                    deps=[t_g_free[gb]], inc="Dg%d" % gb)
                t_ul = sc.dma(
                    "g",
                    lambda g, hc=hc, gb=gb: g.dma_start(
                        AR[:, AR_US[gb]:AR_US[gb] + 2048],
                        wu[hc, :, :]),
                    deps=[t_u_free[gb]], inc="Du%d" % gb)
                if hc >= 2:
                    MM(SM[0:1, :], onesb[:, :],
                       AR[:, AR_H + (hc - 2) * 512:AR_H + (hc - 1) * 512],
                       hc - 2 == 0, False,
                       [t_H[hc - 2]] + ([tok_pb["SM"]] if hc == 2 else []))
                gn = "SA0" if hc % 2 == 0 else "SB0"
                un = "SA1" if hc % 2 == 0 else "SB1"
                pgb = SA0 if hc % 2 == 0 else SB0
                pub = SA1 if hc % 2 == 0 else SB1
                gsil_o = AR_GSIL + (hc % 2) * 512
                for dc in range(NEC):
                    d = [t_gl, tok_pb[gn]] if dc == 0 else []
                    if dc % 4 == 0:
                        d = d + [t_y16s[dc // 2]]
                    t_gm = MM(pgb,
                              AR[:, AR_GS[gb] + dc * 128:AR_GS[gb] + (dc + 1) * 128],
                              C[:, dc * 512:(dc + 1) * 512],
                              dc == 0, dc == NEC - 1, d,
                              inc="T" if dc == NEC - 1 else None)
                for dc in range(NEC):
                    d = [t_ul, tok_pb[un]] if dc == 0 else []
                    if dc % 4 == 0:
                        d = d + [t_y16s[dc // 2]]
                    t_um = MM(pub,
                              AR[:, AR_US[gb] + dc * 128:AR_US[gb] + (dc + 1) * 128],
                              C[:, dc * 512:(dc + 1) * 512],
                              dc == 0, dc == NEC - 1, d,
                              inc="T" if dc == NEC - 1 else None)
                gms.append(t_gm)
                ums.append(t_um)
                t_g_free[gb] = t_gm
                t_u_free[gb] = t_um
                t_sil = sc.st(
                    "s",
                    lambda s, hc=hc, pgb=pgb, gsil_o=gsil_o: s.activation(
                        AR[:, gsil_o:gsil_o + 512], pgb, AF.Silu,
                        scale=sgT[:, hc:hc + 1]),
                    deps=[t_gm, t_hc_last if hc <= 1 else None,
                          t_H[hc - 2] if hc >= 2 else None], inc="S")
                tok_pb[gn] = t_sil
                t_hcv = sc.st(
                    "v",
                    lambda v, hc=hc, pub=pub, gsil_o=gsil_o:
                    v.scalar_tensor_tensor(
                        AR[:, AR_H + hc * 512:AR_H + (hc + 1) * 512],
                        pub, suT[:, hc:hc + 1],
                        AR[:, gsil_o:gsil_o + 512], ALU.mult, ALU.mult),
                    deps=[t_um, t_sil], inc="V")
                tok_pb[un] = t_hcv
                t_hc_last = t_hcv
                t_H.append(t_hcv)

            MM(SM[0:1, :], onesb[:, :],
               AR[:, AR_H + 42 * 512:AR_H + 43 * 512], False, False,
               [t_H[42]])
            t_cs = MM(SM[0:1, :], onesb[:, :],
                      AR[:, AR_H + 43 * 512:AR_H + 44 * 512], False, True,
                      [t_H[43]], inc="T")
            t_rowb = sc.st(
                "s", lambda s: s.copy(rowb[0:1, :], SM[0:1, :]),
                deps=[t_cs, tok_rowb_free], inc="S")
            tok_pb["SM"] = t_rowb

            t_d_free = [[], []]
            pc_dep = [[gms[42]], [gms[43]], [gms[43]], [ums[42]]]
            pc_span = [(0, 2048), (2048, 2816), (2816, 4096), (4096, 5632)]
            pc_sem = ("Dd0", "Dd1", "Dd2", "Dd3")
            t_z = []
            xw_n = []
            for ec in range(NEC):
                dn = "O0" if ec % 2 == 0 else "O1"
                pbd = O0 if ec % 2 == 0 else O1
                hw = [None] * 4
                for p in range(4):
                    c0, c1 = pc_span[p]
                    hw[p] = sc.dma(
                        "g",
                        lambda g, ec=ec, c0=c0, c1=c1: g.dma_start(
                            AR[:, c0:c1], wd[ec, :, c0:c1]),
                        deps=pc_dep[p] + [t_d_free[p // 2]],
                        inc=pc_sem[p])
                if ti + 1 < NT and ec < 4:
                    q0n = (ti + 1) * TT
                    xw_n.append(sc.dma(
                        "g",
                        lambda g, q0n=q0n, k=ec: g.dma_start(
                            C[:, k * 2048:(k + 1) * 2048],
                            bass.AP(x_in, q0n + 4 * k * 128 * 2048,
                                    [[2048, 128], [128 * 2048, 4],
                                     [1, 512]])),
                        deps=[t_u_free[1]]
                             + ([xw_n[ec - 2]] if ec >= 2 else []),
                        inc="Dx%d" % (ec % 2)))
                t_dm = None
                for hc in range(NHC):
                    half = 0 if hc < 22 else 1
                    off = AR_DS[half] + (hc - 22 * half) * 128
                    d = [t_H[hc]]
                    if hc == 0:
                        d += [hw[0], tok_pb[dn]]
                    elif hc == 16:
                        d += [hw[1]]
                    elif hc == 22:
                        d += [hw[2]]
                    elif hc == 32:
                        d += [hw[3]]
                    t_dm = MM(pbd[:, :], AR[:, off:off + 128],
                              AR[:, AR_H + hc * 512:AR_H + (hc + 1) * 512],
                              hc == 0, False, d,
                              inc="T" if hc == 21 else None)
                    if hc == 21:
                        t_d_free[0] = t_dm
                t_dm = MM(pbd[:, :], m128b[0:1, :], rowb[0:1, :],
                          False, True, [t_rowb], inc="T")
                tok_rowb_free = t_dm
                t_d_free[1] = t_dm
                d_z = [t_dm]
                t_zc = sc.st(
                    "v",
                    lambda v, ec=ec, pbd=pbd: v.scalar_tensor_tensor(
                        B[:, ec * 512:(ec + 1) * 512], pbd[:, :],
                        sdT[:, ec:ec + 1], A[:, ec * 512:(ec + 1) * 512],
                        ALU.mult, ALU.add),
                    deps=d_z, inc="V")
                tok_pb[dn] = t_zc
                t_z.append(t_zc)

            if ti + 1 < NT:
                hzf = [t_d_free[1]]
                scrf = hzf
                tok_kh_free = [t_d_free[1], t_d_free[1]]
                tok_vtm_free = [t_d_free[1], t_d_free[1]]
                emit_astart(ti + 1, xw_n, hzf, scrf)

            t_out, _, t_sq2, rows_rd2, t_st2_ln2, t_sub2 = layernorm(
                B, A, t_z, AR_CAT, [t_z[-1]], cast_out=False,
                sa_free=[tok_pb["SA0"], tok_pb["SA1"]],
                sb_free=[tok_pb["SB0"], tok_pb["SB1"]],
                out_in_src=True, f32_stats=True)
            tok_rows_free = rows_rd2
            tok_sc_rd = [t_sub2, t_out[-1]]

            t_st = sc.dma(
                "s",
                lambda s, q0=q0: s.dma_start(
                    bass.AP(y_out, q0,
                            [[2048, 128], [128 * 2048, 16], [1, 512]]),
                    B[:, :]),
                deps=t_out, inc="Dy")

            tok_A_free = list(t_out)
            tok_B_free = [t_st]
            tok_sbinv_free = t_d_free[1]
            tok_cat_free = [t_st2_ln2] * NH

        sc.st("q", lambda q: q.nop(),
              deps=[("Dy", sc.counts.get("Dy", 0))])

        with nc.Block() as block:
            @block.tensor
            def _(t):
                sc.replay("t", t, sems)

            @block.vector
            def _(v):
                sc.replay("v", v, sems)

            @block.scalar
            def _(s):
                sc.replay("s", s, sems)

            @block.gpsimd
            def _(g):
                sc.replay("g", g, sems)

            @block.sync
            def _(q):
                sc.replay("q", q, sems)

    return nc


def _host_layouts(inputs):
    perm = np.array(list(range(0, 64, 2)) + list(range(1, 64, 2)))

    def blk_e(w):
        F = w.shape[1]
        return np.ascontiguousarray(
            w.reshape(16, 128, F).transpose(1, 0, 2).reshape(128, 16 * F))

    def blk_h(w):
        return np.ascontiguousarray(
            w.transpose(1, 0, 2).reshape(128, w.shape[0] * w.shape[2]))

    perm_swap = np.concatenate([perm[32:], perm[:32]])
    zc = np.zeros((E, 64), np.float32)
    zh = np.zeros((NH, 128, 64), np.float32)

    d = {}
    d["wdkv"] = blk_e(np.asarray(inputs["L0_attn_W_dkv"], np.float32))
    d["wdq"] = blk_e(np.asarray(inputs["L0_attn_W_dq"], np.float32))
    wkr = np.asarray(inputs["L0_attn_W_kr"], np.float32)
    d["wkrp"] = blk_e(np.concatenate([zc, wkr[:, perm]], 1))
    d["wkrs"] = blk_e(np.concatenate([zc, wkr[:, perm_swap]], 1))
    wqc = np.asarray(inputs["L0_attn_W_qc"], np.float32)
    wqr = np.asarray(inputs["L0_attn_W_qr"], np.float32)
    d["wqcr"] = blk_h(np.concatenate([wqc, wqr[:, :, perm]], 2))
    d["wqsw"] = blk_h(np.concatenate([zh, wqr[:, :, perm_swap]], 2))
    d["wuk"] = blk_h(np.asarray(inputs["L0_attn_W_uk"], np.float32))
    d["wuv"] = blk_h(np.asarray(inputs["L0_attn_W_uv"], np.float32))
    wo = np.asarray(inputs["L0_attn_W_o"], np.float32)
    d["wo"] = np.ascontiguousarray(
        wo.reshape(16, 128, 16, 128).transpose(2, 1, 0, 3).reshape(16, 128, 2048))
    wgq = np.asarray(inputs["L0_ffn_wq_gate"], np.int32).astype(np.uint8).T
    d["wg"] = np.ascontiguousarray(
        wgq.reshape(16, 128, 44, 128).transpose(2, 1, 0, 3).reshape(44, 128, 2048))
    wuq = np.asarray(inputs["L0_ffn_wq_up"], np.int32).astype(np.uint8).T
    d["wu"] = np.ascontiguousarray(
        wuq.reshape(16, 128, 44, 128).transpose(2, 1, 0, 3).reshape(44, 128, 2048))
    wdn = np.asarray(inputs["L0_ffn_wq_down"], np.int32).astype(np.uint8).T
    d["wd"] = np.ascontiguousarray(
        wdn.reshape(44, 128, 16, 128).transpose(2, 1, 0, 3).reshape(16, 128, 5632))
    d["sg"] = np.ascontiguousarray(
        np.asarray(inputs["L0_ffn_s_gate"], np.float32).reshape(44, 128).T)
    d["su"] = np.ascontiguousarray(
        np.asarray(inputs["L0_ffn_s_up"], np.float32).reshape(44, 128).T)
    d["sd"] = np.ascontiguousarray(
        np.asarray(inputs["L0_ffn_s_down"], np.float32).reshape(16, 128).T)

    half = DR // 2
    freqs = np.exp(np.arange(half, dtype=np.float64) *
                   (-np.log(10000.0) / half))
    ang = np.arange(S, dtype=np.float64)[:, None] * freqs[None, :]
    cos = np.cos(ang).T.astype(np.float32)
    sin = np.sin(ang).T.astype(np.float32)
    rt = np.zeros((128, 2 * S), np.float32)
    rt[64:96, 0:S] = cos
    rt[96:128, 0:S] = cos
    rt[64:96, S:2 * S] = -sin
    rt[96:128, S:2 * S] = sin
    d["rtab"] = rt

    p = np.arange(128)[:, None]
    j = np.arange(896)[None, :]
    d["mks"] = np.where(p <= j - 384, 0.0, -1e30).astype(np.float32)
    d["idm"] = np.eye(128, dtype=np.float32)
    return d


def kernel(L0_attn_x, **weights):
    weights["L0_attn_x"] = L0_attn_x
    shared = _host_layouts(weights)
    x = np.asarray(L0_attn_x, np.float32)
    in_maps = []
    for b in range(8):
        m = dict(shared)
        m["x"] = np.ascontiguousarray(x[b].T)
        in_maps.append(m)
    nc = bass.Bass("TRN2", target_bir_lowering=False)
    build_program(nc)
    res = run_bass_kernel_spmd(nc, in_maps, core_ids=list(range(8)))
    out = np.stack([np.asarray(r["y"]).T for r in res.results], 0)
    return out.astype(np.float32)


if __name__ == "__main__":
    nc_ = bass.Bass("TRN2", target_bir_lowering=False)
    build_program(nc_)
    try:
        n = len(list(nc_.all_instructions()))
    except Exception:
        n = -1
    print("program built, instructions:", n)
